# version 03
# baseline (speedup 1.0000x reference)
import math
from contextlib import ExitStack

import numpy as np
import ml_dtypes

import concourse.bass as bass
import concourse.mybir as mybir
from concourse.bass_utils import run_bass_kernel_spmd

F32 = mybir.dt.float32
BF16 = mybir.dt.bfloat16
F8 = mybir.dt.float8e4
I32 = mybir.dt.int32
AF = mybir.ActivationFunctionType
ALU = mybir.AluOpType
AX = mybir.AxisListType

NCORES = 8
T = 1024
D = 2048
NH = 16
HD = 128
S = 8192
HID = 8192
NCH = 16
NKB = 64
RS = 1.0 / math.sqrt(128.0)
NEG = -1.0e30
MAGIC = 12582912.0
KPR = 2 * D + 1
EXPMASK = 0x7F800000

STAGES = 4
DBG = False
DBG2 = False


class Sem:
    __slots__ = ("h", "n", "name")


class P:
    def __init__(self, nc, stack):
        self.nc = nc
        self.stack = stack
        self.waited = {}

    def sem(self, name):
        s = Sem()
        s.h = self.stack.enter_context(self.nc.semaphore(name))
        s.n = 0
        s.name = name
        return s

    def t(self, inst, s, k=1):
        if s.name in ("sVE", "sSE", "sGE") and s.n > 0:
            inst._wait_ge(s.h, s.n)
        inst.then_inc(s.h, k)
        s.n += k
        return (s, s.n)

    def w(self, eng, *toks):
        best = {}
        for tok in toks:
            if tok is None:
                continue
            s, v = tok
            if v <= 0:
                continue
            if best.get(s.name, (None, 0))[1] < v:
                best[s.name] = (s, v)
        for s, v in best.values():
            key = (id(eng), s.name)
            if self.waited.get(key, 0) >= v:
                continue
            eng.wait_ge(s.h, v)
            self.waited[key] = v

    def cur(self, s):
        return (s, s.n)


def build(stages=STAGES, dbg=False, dbg2=False):
    nc = bass.Bass("TRN2", target_bir_lowering=False)
    g, sy, te, ve, se = nc.gpsimd, nc.sync, nc.tensor, nc.vector, nc.scalar

    din = {}

    def dram_in(name, shape, dtype=F32):
        din[name] = nc.dram_tensor(name, list(shape), dtype, kind="ExternalInput")
        return din[name]

    dram_in("x_fm", [NCH, 128, T])
    dram_in("qpos", [1, T])
    dram_in("qpos_cols", [128, 8])
    dram_in("ident32", [128, 128])
    dram_in("fixmask", [4, 128, 512])
    for w in ("q", "k"):
        dram_in(f"l0_w{w}_codes", [NH, NCH, 128, 128])
        dram_in(f"l0_w{w}_scales", [NH, 64, 128])
    dram_in("l0_wo_codes", [NH, NCH, 128, 128])
    dram_in("l0_wo_scales", [NH, 4, D])
    dram_in("l0_wv_codes", [NCH, 128, D])
    dram_in("l0_wv_scales", [64, D])
    for b in ("bq", "bk", "bv", "bo"):
        dram_in(f"l0_{b}", [128, NCH])
    dram_in("l0_wg0", [64, NCH, 128, 128])
    dram_in("l0_wg1", [64, NCH, 128, 128])
    dram_in("l0_wl", [NCH, 64, 128, 128])
    for w in ("q", "k"):
        dram_in(f"l1_w{w}", [NH, NCH, 128, 128])
    dram_in("l1_wo", [NH, NCH, 128, 128])
    dram_in("l1_wv", [NCH, 128, D])
    for b in ("bq", "bk", "bv", "bo"):
        dram_in(f"l1_{b}", [128, NCH])
    dram_in("ln_w", [128, NCH])
    dram_in("ln_b", [128, NCH])
    dram_in("b1_row", [1, HID])
    dram_in("b2_row", [1, D])
    dram_in("w1_nat_sl", [HID, 256])
    dram_in("w1t_sl", [256, HID])
    dram_in("w2_nat_sl", [D, 1024])
    dram_in("w2t_sl", [1024, D])

    y_out = nc.dram_tensor("y", [NCH, 128, T], F32, kind="ExternalOutput")
    dbg_out = nc.dram_tensor("dbg", [128, 8192], F32, kind="ExternalOutput") if dbg else None
    if dbg2:
        qw1_dump = nc.dram_tensor("qw1_dump", [D, HID], BF16, kind="ExternalOutput")
        sp_dump = nc.dram_tensor("sp_dump", [8, 16, 128, 512], BF16, kind="ExternalOutput")
        w1_dump = nc.dram_tensor("w1_dump", [8, 128, 512], F32, kind="ExternalOutput")
        fa_dump = nc.dram_tensor("fa_dump", [128, 16384], BF16, kind="ExternalOutput")
        xb_dump = nc.dram_tensor("xb_dump", [128, 16384], BF16, kind="ExternalOutput")
        ws_dump = nc.dram_tensor("ws_dump", [128, 8192], BF16, kind="ExternalOutput")
        st_dump = nc.dram_tensor("st_dump", [2, 1024], F32, kind="ExternalOutput")
        hb_dump = nc.dram_tensor("hb_dump", [8, 128, 512], BF16, kind="ExternalOutput")

    kv_pack = [nc.dram_tensor(f"kv_pack{i}", [KPR, T], BF16) for i in (0, 1)]
    kv_ag = [nc.dram_tensor(f"kv_ag{i}", [NCORES * KPR, T], BF16) for i in (0, 1)]
    qw1_sl = nc.dram_tensor("qw1_sl", [256, HID], BF16)
    qw1_ag = nc.dram_tensor("qw1_ag", [D, HID], BF16)
    qw2_sl = nc.dram_tensor("qw2_sl", [1024, D], BF16)
    qw2_ag = nc.dram_tensor("qw2_ag", [HID, D], BF16)
    sc_bounce = nc.dram_tensor("sc_bounce", [32, HID], F32)
    aug_dram = nc.dram_tensor("aug_dram", [NH, 3, T], BF16)
    h2_spill = nc.dram_tensor("h2_spill", [8, 16, 128, 512], BF16)

    stack = ExitStack()
    p = P(nc, stack)

    with stack:
        sb = stack.enter_context
        xr = sb(nc.sbuf_tensor("xr", [128, NCH * T], F32))
        xb = sb(nc.sbuf_tensor("xb", [128, NCH * T], BF16))
        id32 = sb(nc.sbuf_tensor("id32", [128, 128], F32))
        idbf = sb(nc.sbuf_tensor("idbf", [128, 128], BF16))
        ones_cb = sb(nc.sbuf_tensor("ones_cb", [128, 1], BF16))
        ones_rf = sb(nc.sbuf_tensor("ones_rf", [1, 128], F32))
        ones_rb = sb(nc.sbuf_tensor("ones_rb", [1, 128], BF16))
        ones3 = sb(nc.sbuf_tensor("ones3", [3, 128], BF16))
        augrem = sb(nc.sbuf_tensor("augrem", [1, 512], F32))
        bias_sb = sb(nc.sbuf_tensor("bias_sb", [128, NCH * 8], F32))
        lnw_sb = sb(nc.sbuf_tensor("lnw_sb", [128, NCH], F32))
        lnb_sb = sb(nc.sbuf_tensor("lnb_sb", [128, NCH], F32))
        slopes_r = sb(nc.sbuf_tensor("slopes_r", [1, NH], F32))
        slope_c = sb(nc.sbuf_tensor("slope_c", [128, NH], F32))
        qpos_sb = sb(nc.sbuf_tensor("qpos_sb", [1, T], F32))
        qpm_sb = sb(nc.sbuf_tensor("qpm_sb", [128, 8], F32))
        kbrow = sb(nc.sbuf_tensor("kbrow", [1, NKB], F32))
        arow = sb(nc.sbuf_tensor("arow", [128, NKB], F32))
        knorm_r = sb(nc.sbuf_tensor("knorm_r", [1, NH], F32))
        kn2max = sb(nc.sbuf_tensor("kn2max", [1, NH], F32))
        normsc = sb(nc.sbuf_tensor("normsc", [1, 128], BF16))
        flagcol = sb(nc.sbuf_tensor("flagcol", [128, 2 * NKB], F32))
        rowtmp = sb(nc.sbuf_tensor("rowtmp", [1, NKB], F32))
        negc8 = sb(nc.sbuf_tensor("negc8", [128, 1], F32))
        biascol = sb(nc.sbuf_tensor("biascol", [128, 2 * NKB], F32))
        fixbias = sb(nc.sbuf_tensor("fixbias", [128, 8], F32))
        c2t = sb(nc.sbuf_tensor("c2t", [1, 1], F32))
        stat_m = sb(nc.sbuf_tensor("stat_m", [1, T], F32))
        stat_v = sb(nc.sbuf_tensor("stat_v", [1, T], F32))
        mrow2 = sb(nc.sbuf_tensor("mrow2", [1, T], F32))
        x2bf = sb(nc.sbuf_tensor("x2bf", [128, T], BF16))

        pb = [sb(nc.psum_tensor(f"pb{i}", [128, 512], F32)) for i in range(8)]

        sTE = p.sem("sTE")
        sVE = p.sem("sVE")
        sSE = p.sem("sSE")
        sGE = p.sem("sGE")
        sCC = p.sem("sCC")
        dX = p.sem("dX")
        dW0 = p.sem("dW0")
        dW1 = p.sem("dW1")
        dPK = p.sem("dPK")
        dPK1 = p.sem("dPK1")
        dPKV = p.sem("dPKV")
        dKVa = p.sem("dKVa")
        dKVb = p.sem("dKVb")
        dSY = p.sem("dSY")
        dQW0 = p.sem("dQW0")
        dQW1 = p.sem("dQW1")
        ALL_SEMS = (sTE, sVE, sSE, sGE, sCC, dX, dW0, dW1, dPK, dPK1, dPKV,
                    dKVa, dKVb, dSY, dQW0, dQW1)

        def fence():
            toks = [p.cur(s) for s in ALL_SEMS if s.n > 0]
            for eng in (te, ve, se, g, sy):
                p.w(eng, *toks)

        for ci in range(NCH):
            p.t(g.dma_start(xr[:, ci * T:(ci + 1) * T], din["x_fm"][ci]), dX, 16)
            p.t(g.dma_start(xb[:, ci * T:(ci + 1) * T], din["x_fm"][ci]), dX, 16)
        p.t(g.dma_start(id32[:, :], din["ident32"][:, :]), dX, 16)
        p.t(g.dma_start(idbf[:, :], din["ident32"][:, :]), dX, 16)
        p.t(g.dma_start(qpos_sb[:, :], din["qpos"][:, :]), dX, 16)
        for i, b in enumerate(("l0_bq", "l0_bk", "l0_bv", "l0_bo",
                               "l1_bq", "l1_bk", "l1_bv", "l1_bo")):
            p.t(g.dma_start(bias_sb[:, i * NCH:(i + 1) * NCH],
                            din[b][:, :]), dX, 16)
        p.t(g.dma_start(lnw_sb[:, :], din["ln_w"][:, :]), dX, 16)
        p.t(g.dma_start(lnb_sb[:, :], din["ln_b"][:, :]), dX, 16)
        p.t(g.dma_start(qpm_sb[:, :], din["qpos_cols"][:, :]), dX, 16)
        tok_x = p.cur(dX)

        p.t(g.memset(ones_cb[:, :], 1.0), sGE)
        p.t(g.memset(ones_rf[:, :], 1.0), sGE)
        p.t(g.memset(ones_rb[:, :], 1.0), sGE)
        p.t(g.memset(ones3[:, :], 1.0), sGE)
        p.t(g.iota(kbrow[:, :], [[128, NKB]], base=0, channel_multiplier=0,
                   allow_small_or_imprecise_dtypes=True), sGE)
        p.t(g.iota(arow[:, :], [[128, NKB]], base=0, channel_multiplier=1,
                   allow_small_or_imprecise_dtypes=True), sGE)
        p.t(g.iota(slopes_r[:, :], [[1, NH]], base=1, channel_multiplier=0,
                   allow_small_or_imprecise_dtypes=True), sGE)
        tok_ge0 = p.cur(sGE)
        p.w(se, tok_ge0)
        p.t(se.activation(slopes_r[:, :], slopes_r[:, :], AF.Exp,
                          scale=-math.log(2.0) / 2.0), sSE)
        tok_slopes = p.cur(sSE)
        tok_qpm = tok_x
        p.w(te, tok_slopes, tok_ge0)
        mk = p.t(te.matmul(pb[0][:, 0:NH], ones_rf[:, :], slopes_r[:, :],
                           start=True, stop=True, skip_group_check=True), sTE)
        p.w(ve, mk)
        p.t(ve.tensor_copy(slope_c[:, :], pb[0][:, 0:NH]), sVE)
        tok_slopec = p.cur(sVE)

        def ln_stats(tok_ready):
            p.w(te, tok_ready, p.cur(sVE), p.cur(sSE))
            for ci in range(NCH):
                for hf in range(2):
                    mk = p.t(te.matmul(pb[hf][0:1, :], ones_cb[:, :],
                                       xb[:, ci * T + hf * 512:ci * T + (hf + 1) * 512],
                                       start=(ci == 0), stop=(ci == NCH - 1),
                                       skip_group_check=True), sTE)
            tok_x2mm = None
            for ci in range(NCH):
                p.w(ve, tok_ready, tok_x2mm)
                tsq = p.t(ve.tensor_mul(x2bf[:, :], xb[:, ci * T:(ci + 1) * T],
                                        xb[:, ci * T:(ci + 1) * T]), sVE)
                p.w(te, tsq)
                for hf in range(2):
                    tok_x2mm = p.t(te.matmul(pb[2 + hf][0:1, :], ones_cb[:, :],
                                             x2bf[:, hf * 512:(hf + 1) * 512],
                                             start=(ci == 0), stop=(ci == NCH - 1),
                                             skip_group_check=True), sTE)
            tok_mm = p.cur(sTE)
            p.w(ve, tok_mm)
            p.t(ve.tensor_scalar(stat_m[:, 0:512], pb[0][0:1, :], 1.0 / D, None,
                                 ALU.mult), sVE)
            p.t(ve.tensor_scalar(stat_m[:, 512:1024], pb[1][0:1, :], 1.0 / D, None,
                                 ALU.mult), sVE)
            p.t(ve.tensor_scalar(stat_v[:, 0:512], pb[2][0:1, :], 1.0 / D, None,
                                 ALU.mult), sVE)
            p.t(ve.tensor_scalar(stat_v[:, 512:1024], pb[3][0:1, :], 1.0 / D, None,
                                 ALU.mult), sVE)
            p.t(ve.tensor_mul(mrow2[:, :], stat_m[:, :], stat_m[:, :]), sVE)
            p.t(ve.tensor_sub(stat_v[:, :], stat_v[:, :], mrow2[:, :]), sVE)
            p.t(ve.tensor_scalar(stat_v[:, :], stat_v[:, :], 1e-5, None, ALU.add), sVE)
            tok_v = p.cur(sVE)
            p.w(se, tok_v)
            p.t(se.activation(stat_v[:, :], stat_v[:, :], AF.Sqrt), sSE)
            tok_sq = p.cur(sSE)
            p.w(ve, tok_sq)
            p.t(ve.reciprocal(stat_v[:, :], stat_v[:, :]), sVE)
            return p.cur(sVE)

        def ln_apply(tok_stats, refresh_xb=True):
            for ci in range(NCH):
                for hf in range(2):
                    cols = slice(ci * T + hf * 512, ci * T + (hf + 1) * 512)
                    rows = slice(hf * 512, (hf + 1) * 512)
                    bm, bv_ = pb[4 + hf], pb[6 + hf]
                    p.w(te, tok_stats, p.cur(sVE))
                    t1 = p.t(te.matmul(bm[:, :], ones_rf[:, :], stat_m[:, rows],
                                       start=True, stop=True,
                                       skip_group_check=True), sTE)
                    t2 = p.t(te.matmul(bv_[:, :], ones_rf[:, :], stat_v[:, rows],
                                       start=True, stop=True,
                                       skip_group_check=True), sTE)
                    p.w(ve, t1, t2)
                    p.t(ve.tensor_sub(xr[:, cols], xr[:, cols], bm[:, :]), sVE)
                    p.t(ve.tensor_mul(xr[:, cols], xr[:, cols], bv_[:, :]), sVE)
                if refresh_xb:
                    tok_ap = p.cur(sVE)
                    p.w(ve, tok_ap)
                    p.t(ve.tensor_copy(xb[:, ci * T:(ci + 1) * T],
                                       xr[:, ci * T:(ci + 1) * T]), sVE)
            return p.cur(sVE)

        def attention(L, tok_stream):
            pack, ag = kv_pack[L], kv_ag[L]
            is_l0 = (L == 0)
            bq_c = bias_sb[:, (0 + 4 * L) * NCH:(1 + 4 * L) * NCH]
            bk_c = bias_sb[:, (1 + 4 * L) * NCH:(2 + 4 * L) * NCH]
            bv_c = bias_sb[:, (2 + 4 * L) * NCH:(3 + 4 * L) * NCH]
            bo_c = bias_sb[:, (3 + 4 * L) * NCH:(4 + 4 * L) * NCH]
            sfx = f"a{L}"

            with ExitStack() as ast:
                asb = ast.enter_context
                kh = asb(nc.sbuf_tensor(f"kh{sfx}", [128, S], BF16))
                vh = asb(nc.sbuf_tensor(f"vh{sfx}", [128, S], BF16))
                wcod = asb(nc.sbuf_tensor(f"wc{sfx}", [128, D], BF16))
                wscl = asb(nc.sbuf_tensor(f"ws{sfx}", [128, D], BF16))
                wb0 = asb(nc.sbuf_tensor(f"wb0{sfx}", [128, D], BF16))
                wb1 = asb(nc.sbuf_tensor(f"wb1{sfx}", [128, D], BF16))
                sta = asb(nc.sbuf_tensor(f"sta{sfx}", [128, T], BF16))
                stb = asb(nc.sbuf_tensor(f"stb{sfx}", [128, T], BF16))
                qsb = asb(nc.sbuf_tensor(f"q{sfx}", [128, T], BF16))
                psb = asb(nc.sbuf_tensor(f"p{sfx}", [128, 4 * 512], BF16))
                k2bf = asb(nc.sbuf_tensor(f"k2{sfx}", [128, T], BF16))
                osb = asb(nc.sbuf_tensor(f"o{sfx}", [128, T], BF16))
                sfix = asb(nc.sbuf_tensor(f"sf{sfx}", [128, 512], F32))
                kdiag = asb(nc.sbuf_tensor(f"kd{sfx}", [128, 128], BF16))
                fixm_sb = asb(nc.sbuf_tensor(f"fm{sfx}", [128, 4 * 512], BF16))
                aug_sb = asb(nc.sbuf_tensor(f"au{sfx}", [3, T], BF16))
                augr = asb(nc.sbuf_tensor(f"ar{sfx}", [1, 512], F32))
                augbf = asb(nc.sbuf_tensor(f"ab{sfx}", [1, T], BF16))
                augr2 = asb(nc.sbuf_tensor(f"ar2{sfx}", [1, 512], F32))
                rinv_r = asb(nc.sbuf_tensor(f"ri{sfx}", [1, 512], F32))
                rinv_bf = asb(nc.sbuf_tensor(f"rb{sfx}", [1, 512], BF16))
                rinv_bc = asb(nc.sbuf_tensor(f"rc{sfx}", [128, 512], BF16))
                vdiag = asb(nc.sbuf_tensor(f"vd{sfx}", [128, 128], BF16))
                trtile = asb(nc.sbuf_tensor(f"tr{sfx}", [128, 128], BF16))
                tok_fixm = None
                if not is_l0:
                    tok_fixm = p.t(g.dma_start(fixm_sb[:, :],
                                               din["fixmask"].ap()
                                               .rearrange("a p q -> p a q")), dX, 16)
                wfree = {0: None, 1: None}
                deq_free = [None]

                def fetch_colblock(w, h, par):
                    dst = wb0 if par == 0 else wb1
                    wsem = dW0 if par == 0 else dW1
                    p.w(g, wfree[par], deq_free[0])
                    if is_l0:
                        if w == "v":
                            src = din["l0_wv_codes"].ap()[:, :, h * 128:(h + 1) * 128] \
                                .rearrange("c p d -> p c d")
                            sbase = din["l0_wv_scales"].ap()[:, h * 128:(h + 1) * 128] \
                                .rearrange("(c a) d -> a c d", a=4)
                        elif w == "o":
                            src = din["l0_wo_codes"].ap()[h].rearrange("c p d -> p c d")
                            sbase = din["l0_wo_scales"].ap()[h]
                        else:
                            src = din[f"l0_w{w}_codes"].ap()[h].rearrange("c p d -> p c d")
                            sbase = din[f"l0_w{w}_scales"].ap()[h] \
                                .rearrange("(c a) d -> a c d", a=4)
                        t0 = p.t(g.dma_start(wcod[:, :], src), wsem, 16)
                        t1 = None
                        for a in range(4):
                            if w == "o":
                                ss = sbase[a][None, :].broadcast_to([32, D])
                                od = wscl[32 * a:32 * (a + 1), :]
                            else:
                                ss = sbase[a][None, :, :].broadcast_to([32, NCH, 128])
                                od = wscl[32 * a:32 * (a + 1), :] \
                                    .rearrange("p (c d) -> p c d", d=128)
                            t1 = p.t(g.dma_start(od, ss), wsem, 16)
                        p.w(ve, t0, t1)
                        tok = p.t(ve.tensor_mul(dst[:, :], wcod[:, :], wscl[:, :]), sVE)
                        deq_free[0] = tok
                        return tok
                    else:
                        if w == "v":
                            src = din["l1_wv"].ap()[:, :, h * 128:(h + 1) * 128] \
                                .rearrange("c p d -> p c d")
                        elif w == "o":
                            src = din["l1_wo"].ap()[h].rearrange("c p d -> p c d")
                        else:
                            src = din[f"l1_w{w}"].ap()[h].rearrange("c p d -> p c d")
                        return p.t(g.dma_start(dst[:, :], src), wsem, 16)

                def proj_head(wname, h, par, tok_in):
                    tok_w = fetch_colblock(wname, h, par)
                    p.w(te, tok_w, tok_in, p.cur(sSE), p.cur(sVE))
                    dst = wb0 if par == 0 else wb1
                    last = None
                    for hf in range(2):
                        bank = pb[par] if hf == 0 else pb[2 + par]
                        for ci in range(NCH):
                            last = p.t(te.matmul(
                                bank[:, :], dst[:, ci * 128:(ci + 1) * 128],
                                xb[:, ci * T + hf * 512:ci * T + (hf + 1) * 512],
                                start=(ci == 0), stop=(ci == NCH - 1),
                                skip_group_check=True), sTE)
                    wfree[par] = last
                    return last

                def head_norm2(src_bf, dstcol):
                    tk = p.t(ve.tensor_mul(k2bf[:, :], src_bf[:, :], src_bf[:, :]), sVE)
                    p.w(te, tk)
                    t1 = p.t(te.matmul(pb[4][0:1, :], ones_cb[:, :], k2bf[:, 0:512],
                                       start=True, stop=True, skip_group_check=True), sTE)
                    t2 = p.t(te.matmul(pb[5][0:1, :], ones_cb[:, :], k2bf[:, 512:1024],
                                       start=True, stop=True, skip_group_check=True), sTE)
                    p.w(ve, t1, t2)
                    p.t(ve.tensor_reduce(rowtmp[:, 0:1], pb[4][0:1, :], AX.X,
                                         ALU.max), sVE)
                    p.t(ve.tensor_reduce(rowtmp[:, 1:2], pb[5][0:1, :], AX.X,
                                         ALU.max), sVE)
                    return p.t(ve.tensor_max(dstcol, rowtmp[:, 0:1],
                                             rowtmp[:, 1:2]), sVE)

                stg_free = {0: None, 1: None}
                for co in range(NH):
                    par = co % 2
                    stg = sta if par == 0 else stb
                    last = proj_head("k", co, par, tok_stream)
                    p.w(se, last, stg_free[par])
                    p.t(se.activation(stg[:, 0:512], pb[par][:, :], AF.Identity,
                                      bias=bk_c[:, co:co + 1]), sSE)
                    tk = p.t(se.activation(stg[:, 512:1024], pb[2 + par][:, :],
                                           AF.Identity, bias=bk_c[:, co:co + 1]), sSE)
                    p.w(g, tk)
                    stg_free[par] = p.t(
                        g.dma_start(pack.ap()[co * 128:(co + 1) * 128, :],
                                    stg[:, :]), dPK if par == 0 else dPK1, 16)
                    p.w(ve, tk)
                    head_norm2(stg, knorm_r[:, co:co + 1])

                vtr_free = {0: None, 1: None}
                vpk_free = None
                for co in range(NH):
                    par = co % 2
                    stg = sta if par == 0 else stb
                    last = proj_head("v", co, par, tok_stream)
                    p.w(se, last, stg_free[par], vtr_free[par])
                    p.t(se.activation(stg[:, 0:512], pb[par][:, :], AF.Copy), sSE)
                    tk = p.t(se.activation(stg[:, 512:1024], pb[2 + par][:, :],
                                           AF.Copy), sSE)
                    for tt in range(8):
                        p.w(te, tk, p.cur(sVE))
                        tq = p.t(te.transpose(pb[6].ap().bitcast(BF16)[:, 0:128],
                                              stg[:, tt * 128:(tt + 1) * 128],
                                              idbf[:, :]), sTE)
                        vtr_free[par] = tq
                        p.w(ve, tq, vpk_free)
                        tv = p.t(ve.tensor_copy(trtile[:, :],
                                                pb[6].ap().bitcast(BF16)[:, 0:128]), sVE)
                        p.w(g, tv)
                        vpk_free = p.t(g.dma_start(
                            pack.ap()[D + co * 128 + tt * 16:
                                      D + co * 128 + tt * 16 + 16, :]
                            .rearrange("r t -> (r t)")
                            .rearrange("(q d) -> q d", d=128),
                            trtile[:, :]), dPKV, 16)

                p.w(ve, p.cur(sVE), vpk_free)
                p.t(ve.memset(osb[0:1, :], 0.0), sVE)
                tn = p.t(ve.tensor_copy(osb[0:1, 0:NH], knorm_r[:, :]), sVE)
                p.w(g, tn)
                p.t(g.dma_start(pack.ap()[2 * D:2 * D + 1, :],
                                osb[0:1, :]), dPKV, 16)

                p.w(g, p.cur(dPK), p.cur(dPK1), p.cur(dPKV))
                tok_ag = p.t(g.collective_compute(
                    "AllGather", ALU.bypass,
                    replica_groups=[list(range(NCORES))],
                    ins=[pack.ap().opt()], outs=[ag.ap().opt()]), sCC)

                p.w(sy, tok_ag)
                tn = p.t(sy.dma_start(
                    normsc[:, :],
                    ag.ap().rearrange("(r q) t -> r q t", r=NCORES)
                    [:, 2 * D:2 * D + 1, 0:NH]
                    .rearrange("r q t -> q r t")), dSY, 16)
                p.w(ve, tn)
                tok_kn = p.t(ve.tensor_reduce(
                    kn2max[:, :],
                    normsc.ap().rearrange("p (r h) -> p h r", r=NCORES),
                    AX.X, ALU.max), sVE)

                tok_flag = None
                if not is_l0:
                    for gg in range(2):
                        p.w(ve, tok_x, tok_ge0)
                        p.t(ve.tensor_scalar(rowtmp[:, :], kbrow[:, :],
                                             qpos_sb[0:1, 0:1], -512.0 * gg,
                                             ALU.subtract, ALU.add), sVE)
                        tf = p.t(ve.tensor_scalar(rowtmp[:, :], rowtmp[:, :],
                                                  0.0, NEG, ALU.is_ge, ALU.mult), sVE)
                        p.w(te, tf)
                        mk = p.t(te.matmul(pb[7][:, 0:NKB], ones_rf[:, :],
                                           rowtmp[:, :], start=True, stop=True,
                                           skip_group_check=True), sTE)
                        p.w(ve, mk)
                        p.t(ve.tensor_copy(flagcol[:, gg * NKB:(gg + 1) * NKB],
                                           pb[7][:, 0:NKB]), sVE)
                    tok_flag = p.cur(sVE)

                tok_aug_pack = None
                if not is_l0:
                    for h2 in range(NH):
                        p.w(ve, tok_slopes, tok_x)
                        p.t(ve.tensor_scalar(augr[:, 0:T] if False else augr[:, :],
                                             qpos_sb[0:1, 0:512],
                                             slopes_r[0:1, h2:h2 + 1], -1.0 / RS,
                                             ALU.mult, ALU.mult), sVE)
                        p.t(ve.tensor_scalar(augr2[:, :], qpos_sb[0:1, 512:1024],
                                             slopes_r[0:1, h2:h2 + 1], -1.0 / RS,
                                             ALU.mult, ALU.mult), sVE)
                        for j in range(3):
                            p.w(ve, tok_aug_pack)
                            p.t(ve.tensor_copy(augbf[:, 0:512], augr[:, :]), sVE)
                            p.t(ve.tensor_copy(augbf[:, 512:1024], augr2[:, :]), sVE)
                            tj = p.cur(sVE)
                            p.w(g, tj)
                            tok_aug_pack = p.t(
                                g.dma_start(aug_dram.ap()[h2, j], augbf[:, :]),
                                dPKV, 16)
                            if j < 2:
                                p.t(ve.tensor_sub(augr[:, :], augr[:, :],
                                                  augbf[:, 0:512]), sVE)
                                p.t(ve.tensor_sub(augr2[:, :], augr2[:, :],
                                                  augbf[:, 512:1024]), sVE)

                def load_kv_half(h, half, ksem):
                    for r in range(4 * half, 4 * half + 4):
                        base = r * KPR
                        p.t(sy.dma_start(
                            kh[:, r * T:(r + 1) * T],
                            ag.ap()[base + h * 128:base + (h + 1) * 128, :]),
                            ksem, 16)
                        p.t(sy.dma_start(
                            vh.ap()[:, r * T:(r + 1) * T]
                            .rearrange("p (tb d) -> p tb d", d=128),
                            ag.ap()[base + D + h * 128:base + D + (h + 1) * 128, :]
                            .rearrange("r t -> (r t)")
                            .rearrange("(tb tp d) -> tp tb d", tb=8, d=128)),
                            ksem, 16)
                    return p.cur(ksem)

                p.w(sy, tok_ag)
                kv_tok = [load_kv_half(0, 0, dKVa), load_kv_half(0, 1, dKVb)]

                exp_tok = {}
                for h in range(NH):
                    p.w(te, p.cur(sVE), p.cur(sSE))
                    last = proj_head("q", h, 0, tok_stream)
                    p.w(se, last)
                    p.t(se.activation(qsb[:, 0:512], pb[0][:, :], AF.Identity,
                                      bias=bq_c[:, h:h + 1]), sSE)
                    tok_q = p.t(se.activation(qsb[:, 512:1024], pb[2][:, :],
                                              AF.Identity, bias=bq_c[:, h:h + 1]), sSE)
                    p.w(ve, tok_q, tok_kn)
                    head_norm2(qsb, c2t[:, :])
                    p.t(ve.tensor_mul(c2t[:, :], c2t[:, :], kn2max[:, h:h + 1]), sVE)
                    tok_c2 = p.cur(sVE)
                    p.w(se, tok_c2)
                    tc = p.t(se.activation(c2t[:, :], c2t[:, :], AF.Sqrt,
                                           scale=(1.02 * RS) ** 2), sSE)
                    p.w(ve, tc)
                    tnc = p.t(ve.tensor_scalar(c2t[:, :], c2t[:, :], -1.0, 6.0,
                                               ALU.mult, ALU.add), sVE)
                    p.w(te, tnc)
                    mk = p.t(te.matmul(pb[6][:, 0:1], ones_rf[:, :], c2t[:, :],
                                       start=True, stop=True,
                                       skip_group_check=True), sTE)
                    p.w(ve, mk)
                    tok_nc = p.t(ve.tensor_copy(negc8[:, :], pb[6][:, 0:1]), sVE)

                    p.w(ve, tok_nc, tok_slopec, tok_flag, p.cur(sSE))
                    if is_l0:
                        for gg in range(2):
                            p.t(ve.tensor_copy(
                                biascol[:, gg * NKB:(gg + 1) * NKB],
                                negc8.ap().broadcast_to([128, NKB])), sVE)
                    else:
                        for gg in range(2):
                            p.t(ve.scalar_tensor_tensor(
                                biascol[:, gg * NKB:(gg + 1) * NKB],
                                arow[:, :], slope_c[:, h:h + 1],
                                negc8.ap().broadcast_to([128, NKB]),
                                ALU.mult, ALU.add), sVE)
                            p.t(ve.tensor_add(
                                biascol[:, gg * NKB:(gg + 1) * NKB],
                                biascol[:, gg * NKB:(gg + 1) * NKB],
                                flagcol[:, gg * NKB:(gg + 1) * NKB]), sVE)
                        p.w(ve, tok_qpm)
                        p.t(ve.scalar_tensor_tensor(
                            fixbias[:, :], qpm_sb[:, :], slope_c[:, h:h + 1],
                            negc8.ap().broadcast_to([128, 8]),
                            ALU.mult, ALU.add), sVE)
                    tok_bias = p.cur(sVE)
                    if not is_l0:
                        p.w(sy, tok_aug_pack, p.cur(sTE))
                        tok_aug = p.t(sy.dma_start(
                            aug_sb[:, :],
                            aug_dram.ap()[h].rearrange("a t -> a t")), dSY, 16)
                        p.w(te, tok_aug)
                    p.w(se, tok_bias)
                    p.w(te, tok_bias)

                    for kb in range(NKB):
                        ktok = kv_tok[kb // 32]
                        for gg in range(2):
                            sbank = pb[2 * gg + (kb % 2)]
                            p.w(te, ktok, exp_tok.get((gg, kb % 2)), tok_q)
                            if not is_l0:
                                p.t(te.matmul(sbank[:, :], ones3[:, :],
                                              aug_sb[:, gg * 512:(gg + 1) * 512],
                                              start=True, stop=False,
                                              skip_group_check=True), sTE)
                            ts_ = p.t(te.matmul(sbank[:, :],
                                                kh[:, kb * 128:(kb + 1) * 128],
                                                qsb[:, gg * 512:(gg + 1) * 512],
                                                start=is_l0, stop=True,
                                                skip_group_check=True), sTE)
                            pslot = slice((2 * gg + kb % 2) * 512,
                                          (2 * gg + kb % 2 + 1) * 512)
                            p.w(se, ts_)
                            tke = p.t(se.activation(
                                psb[:, pslot], sbank[:, :], AF.Exp,
                                bias=biascol[:, gg * NKB + kb:gg * NKB + kb + 1],
                                scale=RS), sSE)
                            exp_tok[(gg, kb % 2)] = tke
                            p.w(te, tke)
                            p.t(te.matmul(pb[4 + gg][:, :],
                                          vh[:, kb * 128:(kb + 1) * 128],
                                          psb[:, pslot], start=(kb == 0),
                                          stop=(is_l0 and kb == NKB - 1),
                                          skip_group_check=True), sTE)
                            p.t(te.matmul(pb[6 + gg][0:1, :], ones_cb[:, :],
                                          psb[:, pslot], start=(kb == 0),
                                          stop=(is_l0 and kb == NKB - 1),
                                          skip_group_check=True), sTE)
                        if kb == 31 and h + 1 < NH:
                            p.w(sy, p.cur(sTE))
                            kv_tok[0] = load_kv_half(h + 1, 0, dKVa)
                    if h + 1 < NH:
                        p.w(sy, p.cur(sTE))
                        kv_tok[1] = load_kv_half(h + 1, 1, dKVb)

                    if not is_l0:
                        for gg in range(2):
                            for i in range(4):
                                tb = 4 * gg + i
                                p.w(sy, p.cur(sTE))
                                tdk = p.t(sy.dma_start(
                                    kdiag[:, :],
                                    pack.ap()[h * 128:(h + 1) * 128,
                                              tb * 128:(tb + 1) * 128]), dSY, 16)
                                tdv = p.t(sy.dma_start(
                                    vdiag[:, :],
                                    pack.ap()[D + h * 128 + tb * 16:
                                              D + h * 128 + tb * 16 + 16, :]
                                    .rearrange("r t -> (r t)")
                                    .rearrange("(tp d) -> tp d", d=128)), dSY, 16)
                                p.w(te, tdk, tdv, tok_q, exp_tok.get((gg, 0)))
                                p.t(te.matmul(pb[2 * gg][:, :], ones3[:, :],
                                              aug_sb[:, gg * 512:(gg + 1) * 512],
                                              start=True, stop=False,
                                              skip_group_check=True), sTE)
                                ts_ = p.t(te.matmul(pb[2 * gg][:, :], kdiag[:, :],
                                                    qsb[:, gg * 512:(gg + 1) * 512],
                                                    start=False, stop=True,
                                                    skip_group_check=True), sTE)
                                p.w(ve, ts_, p.cur(sSE), tok_fixm)
                                tm = p.t(ve.tensor_add(
                                    sfix[:, :], pb[2 * gg][:, :],
                                    fixm_sb[:, i * 512:(i + 1) * 512]), sVE)
                                p.w(se, tm)
                                tke = p.t(se.activation(
                                    psb[:, (2 * gg) * 512:(2 * gg + 1) * 512],
                                    sfix[:, :], AF.Exp,
                                    bias=fixbias[:, tb:tb + 1], scale=RS), sSE)
                                exp_tok[(gg, 0)] = tke
                                p.w(te, tke, tdv)
                                p.t(te.matmul(pb[4 + gg][:, :], vdiag[:, :],
                                              psb[:, (2 * gg) * 512:
                                                  (2 * gg + 1) * 512],
                                              start=False,
                                              stop=(gg == 1 and i == 3),
                                              skip_group_check=True), sTE)
                                p.t(te.matmul(pb[6 + gg][0:1, :], ones_cb[:, :],
                                              psb[:, (2 * gg) * 512:
                                                  (2 * gg + 1) * 512],
                                              start=False,
                                              stop=(gg == 1 and i == 3),
                                              skip_group_check=True), sTE)
                    tok_pv = p.cur(sTE)

                    prev_rinv_mm = None
                    for gg in range(2):
                        p.w(ve, tok_pv, prev_rinv_mm)
                        p.t(ve.reciprocal(rinv_r[:, :], pb[6 + gg][0:1, :]), sVE)
                        tr_ = p.t(ve.tensor_copy(rinv_bf[:, :], rinv_r[:, :]), sVE)
                        p.w(te, tr_)
                        mk = p.t(te.matmul(pb[2 * gg][:, :], ones_rb[:, :],
                                           rinv_bf[:, :], start=True, stop=True,
                                           skip_group_check=True), sTE)
                        prev_rinv_mm = mk
                        p.w(ve, mk)
                        p.t(ve.tensor_copy(rinv_bc[:, :], pb[2 * gg][:, :]), sVE)
                        tro = p.t(ve.tensor_mul(
                            k2bf[:, gg * 512:(gg + 1) * 512],
                            pb[4 + gg][:, :], rinv_bc[:, :]), sVE)
                        p.w(se, tro)
                        p.t(se.activation(osb[:, gg * 512:(gg + 1) * 512],
                                          k2bf[:, gg * 512:(gg + 1) * 512],
                                          AF.Identity, bias=bv_c[:, h:h + 1]), sSE)
                    tok_o = p.cur(sSE)

                    tok_wo = fetch_colblock("o", h, 1)
                    p.w(te, tok_wo, tok_o, p.cur(sVE))
                    for oc in range(NCH):
                        for gg in range(2):
                            bank = pb[2 * gg + (oc % 2)]
                            p.w(te, exp_tok.get(("wo", oc % 2, gg)))
                            mk = p.t(te.matmul(bank[:, :],
                                               wb1[:, oc * 128:(oc + 1) * 128],
                                               osb[:, gg * 512:(gg + 1) * 512],
                                               start=True, stop=True,
                                               skip_group_check=True), sTE)
                            p.w(ve, mk)
                            cols = slice(oc * T + gg * 512, oc * T + (gg + 1) * 512)
                            ta = p.t(ve.tensor_add(xr[:, cols], xr[:, cols],
                                                   bank[:, :]), sVE)
                            exp_tok[("wo", oc % 2, gg)] = ta
                    wfree[1] = p.cur(sTE)

                if dbg and L == 0:
                    with nc.sbuf_tensor("dbgs", [128, 4096], F32) as dbgs:
                        p.w(ve, p.cur(sVE), p.cur(sSE), p.cur(sTE))
                        p.t(ve.memset(dbgs[:, :], 0.0), sVE)
                        p.t(ve.tensor_copy(dbgs[:, 0:1024], xr[:, 0:1024]), sVE)
                        p.t(ve.tensor_copy(dbgs[:, 1024:2048], osb[:, :]), sVE)
                        p.t(ve.tensor_copy(dbgs[0:1, 2048:2064], kn2max[:, :]), sVE)
                        p.t(ve.tensor_copy(dbgs[0:1, 2064:2080], knorm_r[:, :]), sVE)
                        p.t(ve.tensor_copy(dbgs[:, 2080:2081], negc8[:, :]), sVE)
                        p.t(ve.tensor_copy(dbgs[0:1, 2100:2612], rinv_r[:, :]), sVE)
                        p.t(ve.tensor_copy(dbgs[:, 3072:4096], qsb[:, :]), sVE)
                        p.t(ve.tensor_copy(dbgs[:, 2612:2676], biascol[:, 0:64]), sVE)
                        p.t(ve.tensor_copy(dbgs[0:1, 2676:2804],
                                           normsc[:, :]), sVE)
                        tdbg = p.cur(sVE)
                        p.w(g, tdbg)
                        p.t(g.dma_start(dbg_out.ap()[:, 0:4096], dbgs[:, :]), dX, 16)
                        p.w(g, p.cur(dX))
                p.w(ve, p.cur(sVE))
                for ci in range(NCH):
                    p.t(ve.tensor_scalar(xr[:, ci * T:(ci + 1) * T],
                                         xr[:, ci * T:(ci + 1) * T],
                                         bo_c[:, ci:ci + 1], None, ALU.add), sVE)
                tok_res = p.cur(sVE)
                for ci in range(NCH):
                    p.w(ve, tok_res)
                    p.t(ve.tensor_copy(xb[:, ci * T:(ci + 1) * T],
                                       xr[:, ci * T:(ci + 1) * T]), sVE)
            return p.cur(sVE)

        def ffn0(tok_stream, dbg=False):
            with (
                nc.sbuf_tensor("xf8", [128, NCH * T], F8) as xf8,
                nc.sbuf_tensor("actf8", [128, 64 * 512], F8) as actf8,
                nc.sbuf_tensor("w8a", [128, D], F8) as w8a,
                nc.sbuf_tensor("w8b", [128, D], F8) as w8b,
                nc.sbuf_tensor("w8c", [128, D], F8) as w8c,
                nc.sbuf_tensor("w8d", [128, D], F8) as w8d,
                nc.sbuf_tensor("gbf", [128, 512], BF16) as gbf,
                nc.sbuf_tensor("fbf", [128, 512], BF16) as fbf,
                nc.sbuf_tensor("abf", [128, 512], BF16) as abf,
            ):
                for ci in range(NCH):
                    p.w(ve, tok_stream)
                    p.t(ve.tensor_copy(xf8[:, ci * T:(ci + 1) * T],
                                       xr[:, ci * T:(ci + 1) * T]), sVE)
                tok_x8 = p.cur(sVE)

                wg_free = {0: None, 1: None}
                wf_free = {0: None, 1: None}
                ev = {}
                tok_wl_done = None
                for tg in range(2):
                    for hc in range(64):
                        par = hc % 2
                        wsem = dW0 if par == 0 else dW1
                        wg_ = w8a if par == 0 else w8b
                        wf_ = w8c if par == 0 else w8d
                        p.w(g, wg_free[par])
                        t0 = p.t(g.dma_start(
                            wg_[:, :], din["l0_wg0"].ap()[hc]
                            .rearrange("c p q -> p c q")), wsem, 16)
                        p.w(g, wf_free[par])
                        t1 = p.t(g.dma_start(
                            wf_[:, :], din["l0_wg1"].ap()[hc]
                            .rearrange("c p q -> p c q")), wsem, 16)
                        bg, bf_ = pb[par], pb[2 + par]
                        p.w(te, t0, t1, tok_x8, ev.get(("g", par)), ev.get(("f", par)))
                        gm = fm = None
                        for ci in range(NCH):
                            gm = p.t(te.matmul(
                                bg[:, :], wg_[:, ci * 128:(ci + 1) * 128],
                                xf8[:, ci * T + tg * 512:ci * T + (tg + 1) * 512],
                                start=(ci == 0), stop=(ci == NCH - 1),
                                skip_group_check=True), sTE)
                        for ci in range(NCH):
                            fm = p.t(te.matmul(
                                bf_[:, :], wf_[:, ci * 128:(ci + 1) * 128],
                                xf8[:, ci * T + tg * 512:ci * T + (tg + 1) * 512],
                                start=(ci == 0), stop=(ci == NCH - 1),
                                skip_group_check=True), sTE)
                        wg_free[par] = gm
                        wf_free[par] = fm
                        p.w(se, gm, p.cur(sVE))
                        p.t(se.activation(gbf[:, :], bg[:, :], AF.Copy), sSE)
                        p.t(se.activation(gbf[:, :], gbf[:, :],
                                          AF.Gelu_apprx_tanh), sSE)
                        p.w(se, fm)
                        tf = p.t(se.activation(fbf[:, :], bf_[:, :], AF.Copy), sSE)
                        ev[("g", par)] = tf
                        ev[("f", par)] = tf
                        p.w(ve, tf, tok_wl_done)
                        p.t(ve.tensor_mul(abf[:, :], gbf[:, :], fbf[:, :]), sVE)
                        ev[("a", hc)] = p.t(ve.tensor_copy(
                            actf8[:, hc * 512:(hc + 1) * 512], abf[:, :]), sVE)
                        if dbg and tg == 0 and hc == 0:
                            p.w(ve, p.cur(sVE), p.cur(sSE))
                            p.t(ve.tensor_copy(xf8[:, T:T + 1024].bitcast(BF16)
                                               .bitcast(F8)[:, 0:1024]
                                               if False else
                                               actf8[:, 63 * 512:64 * 512],
                                               abf[:, :]), sVE)
                            with nc.sbuf_tensor("dbg0", [128, 2048], F32) as dbg0:
                                p.t(ve.tensor_copy(dbg0[:, 0:512], gbf[:, :]), sVE)
                                p.t(ve.tensor_copy(dbg0[:, 512:1024], fbf[:, :]), sVE)
                                p.t(ve.tensor_copy(dbg0[:, 1024:1536], abf[:, :]), sVE)
                                p.t(ve.tensor_copy(dbg0[:, 1536:2048],
                                                   xf8[:, 0:512]), sVE)
                                tdbg = p.cur(sVE)
                                p.w(g, tdbg)
                                p.t(g.dma_start(dbg_out.ap()[:, 4096:6144],
                                                dbg0[:, :]), dX, 16)
                                p.w(g, p.cur(dX))
                    for oc in range(NCH):
                        par = oc % 2
                        wsem = dW0 if par == 0 else dW1
                        wl_ = w8a if par == 0 else w8b
                        bank = pb[4 + par]
                        om = None
                        for hg in range(4):
                            p.w(g, wg_free[par])
                            tw = p.t(g.dma_start(
                                wl_[:, :],
                                din["l0_wl"].ap()[oc, 16 * hg:16 * (hg + 1)]
                                .rearrange("c p q -> p c q")), wsem, 16)
                            p.w(te, tw, ev.get(("o", par)))
                            for k in range(16):
                                hc = hg * 16 + k
                                p.w(te, ev[("a", hc)])
                                om = p.t(te.matmul(
                                    bank[:, :], wl_[:, k * 128:(k + 1) * 128],
                                    actf8[:, hc * 512:(hc + 1) * 512],
                                    start=(hc == 0), stop=(hc == 63),
                                    skip_group_check=True), sTE)
                            wg_free[par] = om
                        p.w(se, om, p.cur(sVE))
                        to = p.t(se.activation(gbf[:, :],
                                               bank[:, :], AF.Copy), sSE)
                        p.w(ve, to)
                        cols = slice(oc * T + tg * 512, oc * T + (tg + 1) * 512)
                        ev[("o", par)] = p.t(ve.tensor_add(
                            xr[:, cols], xr[:, cols], gbf[:, :]), sVE)
                    tok_wl_done = p.cur(sTE)
                tok_res = p.cur(sVE)
                for ci in range(NCH):
                    p.w(ve, tok_res)
                    p.t(ve.tensor_copy(xb[:, ci * T:(ci + 1) * T],
                                       xr[:, ci * T:(ci + 1) * T]), sVE)
            return p.cur(sVE)

        def prep_quant():
            with (
                nc.sbuf_tensor("pnat", [128, 1024], F32) as pnat,
                nc.sbuf_tensor("pamax", [128, 32], F32) as pamax,
                nc.sbuf_tensor("pscT", [32, 128], F32) as pscT,
                nc.sbuf_tensor("papp", [128, 2048], F32) as papp,
                nc.sbuf_tensor("pbc", [128, 2048], F32) as pbc,
                nc.sbuf_tensor("prbc", [128, 2048], F32) as prbc,
                nc.sbuf_tensor("pout", [128, 2048], BF16) as pout,
            ):
                tok_cc = None
                bounce_tok = [None]
                for (nat, tsl, qsl, qag, KSL, OD) in (
                    ("w1_nat_sl", "w1t_sl", qw1_sl, qw1_ag, 256, HID),
                    ("w2_nat_sl", "w2t_sl", qw2_sl, qw2_ag, 1024, D),
                ):
                    nblk = KSL // 32
                    prev_rd = None
                    tr_prev = None
                    for ot in range(OD // 128):
                        p.w(g, prev_rd)
                        tw = p.t(g.dma_start(
                            pnat[:, 0:KSL],
                            din[nat].ap()[ot * 128:(ot + 1) * 128, :]), dX, 16)
                        p.w(ve, tw, tr_prev)
                        prev_rd = p.t(ve.tensor_reduce(
                            pamax[:, 0:nblk],
                            pnat.ap()[:, 0:KSL].rearrange("p (b w) -> p b w", w=32),
                            AX.X, ALU.max, apply_absolute_value=True), sVE)
                        p.t(ve.tensor_scalar(
                            pamax.ap()[:, 0:nblk].bitcast(I32),
                            pamax.ap()[:, 0:nblk].bitcast(I32),
                            EXPMASK, None, ALU.bitwise_and), sVE)
                        tsc = p.t(ve.tensor_scalar(pamax[:, 0:nblk], pamax[:, 0:nblk],
                                                   0.015625, None, ALU.mult), sVE)
                        p.w(te, tsc)
                        tq = p.t(te.transpose(pb[6][0:nblk, 0:128],
                                              pamax[:, 0:nblk],
                                              id32[:, :]), sTE)
                        p.w(ve, tq, bounce_tok[0])
                        tr_prev = p.t(ve.tensor_copy(
                            pscT[0:nblk, 0:128],
                            pb[6][0:nblk, 0:128]), sVE)
                        p.w(g, tr_prev)
                        bounce_tok[0] = p.t(g.dma_start(
                            sc_bounce.ap()[0:nblk, ot * 128:(ot + 1) * 128],
                            pscT[0:nblk, 0:128]), dPK, 16)
                    tb = bounce_tok[0]
                    agt = []
                    for kc in range(KSL // 128):
                        for oh in range(OD // 2048):
                            oco = slice(oh * 2048, (oh + 1) * 2048)
                            p.w(g, tb, agt[-1] if agt else None)
                            t0 = p.t(g.dma_start(
                                papp[:, :],
                                din[tsl].ap()[kc * 128:(kc + 1) * 128, oco]), dX, 16)
                            t1 = None
                            for a in range(4):
                                t1 = p.t(g.dma_start(
                                    pbc[32 * a:32 * (a + 1), :],
                                    sc_bounce.ap()[kc * 4 + a][None, :][:, oco]
                                    .broadcast_to([32, 2048])), dX, 16)
                            p.w(ve, t0, t1)
                            p.t(ve.reciprocal(prbc[:, :], pbc[:, :]), sVE)
                            p.t(ve.tensor_mul(papp[:, :], papp[:, :], prbc[:, :]), sVE)
                            p.t(ve.tensor_scalar(papp[:, :], papp[:, :],
                                                 MAGIC, -MAGIC, ALU.add, ALU.add), sVE)
                            tq = p.t(ve.scalar_tensor_tensor(
                                pout[:, :], papp[:, :], 127.0, pbc[:, :],
                                ALU.min, ALU.mult), sVE)
                            p.w(g, tq)
                            agt.append(p.t(g.dma_start(
                                qsl.ap()[kc * 128:(kc + 1) * 128, oco],
                                pout[:, :]), dPK, 16))
                    p.w(g, agt[-1])
                    tok_cc = p.t(g.collective_compute(
                        "AllGather", ALU.bypass,
                        replica_groups=[list(range(NCORES))],
                        ins=[qsl.ap().opt()], outs=[qag.ap().opt()]), sCC)
            return tok_cc

        def ffn1(tok_stream, tok_prep, dbg2=False):
            with (
                nc.sbuf_tensor("fA", [128, NCH * T], BF16) as fA,
                nc.sbuf_tensor("fB", [128, NCH * T], BF16) as fB,
                nc.sbuf_tensor("cbf", [128, D], BF16) as cbf,
                nc.sbuf_tensor("qt32", [128, D // 2], F32) as qt32,
                nc.sbuf_tensor("qsc", [128, 64], F32) as qsc,
                nc.sbuf_tensor("qrsc", [128, 64], F32) as qrsc,
                nc.sbuf_tensor("h1t", [128, 512], BF16) as h1t,
                nc.sbuf_tensor("h2q", [128, 512], BF16) as h2q,
                nc.sbuf_tensor("h2f", [128, 512], F32) as h2f,
                nc.sbuf_tensor("h2sc", [128, 16], F32) as h2sc,
                nc.sbuf_tensor("y2tm", [128, 2 * D], BF16) as y2tm,
                nc.sbuf_tensor("b1t", [1, 512], BF16) as b1t,
                nc.sbuf_tensor("bbc", [128, 512], BF16) as bbc,
            ):
                if dbg2:
                    p.w(g, tok_stream)
                    p.t(g.dma_start(xb_dump.ap()[:, :], xb[:, :]), dX, 16)
                tok_stats = ln_stats(tok_stream)
                if dbg2:
                    p.w(g, tok_stats)
                    p.t(g.dma_start(st_dump.ap()[0:1, :], stat_m[:, :]), dX, 16)
                    p.t(g.dma_start(st_dump.ap()[1:2, :], stat_v[:, :]), dX, 16)
                tr_tok = None
                for ci in range(NCH):
                    for hf in range(2):
                        cols = slice(ci * T + hf * 512, ci * T + (hf + 1) * 512)
                        rows = slice(hf * 512, (hf + 1) * 512)
                        ccols = slice(hf * 512, (hf + 1) * 512)
                        bm, bv_ = pb[0 + hf], pb[2 + hf]
                        p.w(te, tok_stats, p.cur(sVE), p.cur(sSE))
                        t1 = p.t(te.matmul(bm[:, :], ones_rf[:, :], stat_m[:, rows],
                                           start=True, stop=True,
                                           skip_group_check=True), sTE)
                        t2 = p.t(te.matmul(bv_[:, :], ones_rf[:, :], stat_v[:, rows],
                                           start=True, stop=True,
                                           skip_group_check=True), sTE)
                        p.w(ve, t1, t2)
                        p.t(ve.tensor_sub(cbf[:, ccols], xb[:, cols], bm[:, :]), sVE)
                        tap = p.t(ve.tensor_mul(cbf[:, ccols], cbf[:, ccols],
                                                bv_[:, :]), sVE)
                        p.w(se, tap)
                        p.t(se.activation(cbf[:, ccols], cbf[:, ccols], AF.Identity,
                                          bias=lnb_sb[:, ci:ci + 1],
                                          scale=lnw_sb[:, ci:ci + 1]), sSE)
                    th = p.cur(sSE)
                    for tt in range(8):
                        p.w(te, th, tr_tok)
                        tq = p.t(te.transpose(
                            pb[4 + tt % 2].ap().bitcast(BF16)[:, 0:128],
                            cbf[:, tt * 128:(tt + 1) * 128],
                            idbf[:, :]), sTE)
                        p.w(ve, tq)
                        tr_tok = p.t(ve.tensor_copy(
                            fB[:, tt * D + ci * 128:tt * D + (ci + 1) * 128],
                            pb[4 + tt % 2].ap().bitcast(BF16)[:, 0:128]), sVE)
                for tt in range(8):
                    p.w(ve, tr_tok)
                    p.t(ve.tensor_reduce(
                        qsc[:, :],
                        fB.ap()[:, tt * D:(tt + 1) * D]
                        .rearrange("p (b w) -> p b w", w=32),
                        AX.X, ALU.max, apply_absolute_value=True), sVE)
                    p.t(ve.tensor_scalar(qsc.ap().bitcast(I32), qsc.ap().bitcast(I32),
                                         EXPMASK, None, ALU.bitwise_and), sVE)
                    p.t(ve.tensor_scalar(qsc[:, :], qsc[:, :], 0.015625, None,
                                         ALU.mult), sVE)
                    tsc = p.t(ve.reciprocal(qrsc[:, :], qsc[:, :]), sVE)
                    p.w(ve, tsc)
                    for hh in range(2):
                        hc2 = slice(hh * 1024, (hh + 1) * 1024)
                        hb = slice(hh * 32, (hh + 1) * 32)
                        p.t(ve.tensor_mul(
                            qt32.ap().rearrange("p (b w) -> p b w", w=32),
                            fB.ap()[:, tt * D:(tt + 1) * D][:, hc2]
                            .rearrange("p (b w) -> p b w", w=32),
                            qrsc.ap()[:, hb][:, :, None]
                            .broadcast_to([128, 32, 32])), sVE)
                        p.t(ve.tensor_scalar(qt32[:, :], qt32[:, :], MAGIC, -MAGIC,
                                             ALU.add, ALU.add), sVE)
                        tq = p.t(ve.scalar_tensor_tensor(
                            cbf.ap()[:, 0:D][:, hc2]
                            .rearrange("p (b w) -> p b w", w=32),
                            qt32.ap().rearrange("p (b w) -> p b w", w=32), 127.0,
                            qsc.ap()[:, hb][:, :, None]
                            .broadcast_to([128, 32, 32]),
                            ALU.min, ALU.mult), sVE)
                    for ci in range(NCH):
                        p.w(te, tq, p.cur(sVE))
                        tk = p.t(te.transpose(
                            pb[4 + ci % 2].ap().bitcast(BF16)[:, 0:128],
                            cbf[:, ci * 128:(ci + 1) * 128],
                            idbf[:, :]), sTE)
                        p.w(ve, tk)
                        p.t(ve.tensor_copy(
                            fA[:, ci * T + tt * 128:ci * T + (tt + 1) * 128],
                            pb[4 + ci % 2].ap().bitcast(BF16)[:, 0:128]), sVE)
                tok_qh = p.cur(sVE)
                if dbg2:
                    p.w(g, tok_qh)
                    p.t(g.dma_start(fa_dump.ap()[:, :], fA[:, :]), dX, 16)

                p.w(sy, tok_prep)
                wslot = [fB[:, 0:16 * 512], fB[:, 16 * 512:32 * 512]]
                qw_free = {0: p.cur(sVE), 1: p.cur(sVE)}
                b1_free = [None]
                b2_free = [None]
                wtoks = {}
                sp_tok = {}
                bank_free = {0: None, 1: None}
                h1t_free = None
                h2q_free = None
                ws_dump_tok = None
                bbc_dump = [None]
                h1t_dump = [None]
                for og in range(16):
                    par = og % 2
                    qsem = dQW0 if par == 0 else dQW1
                    p.w(sy, qw_free[par], tok_prep,
                        ws_dump_tok if og == 2 else None)
                    wtoks[par] = p.t(sy.dma_start(
                        wslot[par].rearrange("p (c q) -> p c q", q=512),
                        qw1_ag.ap()[:, og * 512:(og + 1) * 512]
                        .rearrange("(c p) q -> p c q", p=128)), qsem, 16)
                    if dbg2 and og == 0:
                        p.w(g, wtoks[par])
                        ws_dump_tok = p.t(
                            g.dma_start(ws_dump.ap()[:, :], wslot[0][:, :]),
                            dX, 16)
                    p.w(g, b1_free[0])
                    tb1 = p.t(g.dma_start(
                        b1t[:, :], din["b1_row"].ap()[:, og * 512:(og + 1) * 512]),
                        dX, 16)
                    p.w(te, tb1, b1_free[0])
                    tbm = p.t(te.matmul(pb[6][:, :], ones_rb[:, :], b1t[:, :],
                                        start=True, stop=True,
                                        skip_group_check=True), sTE)
                    p.w(ve, tbm, bbc_dump[0])
                    tb1bc = p.t(ve.tensor_copy(bbc[:, :], pb[6][:, :]), sVE)
                    b1_free[0] = tbm
                    if dbg2 and og in (0, 1):
                        p.w(g, tb1bc)
                        bbc_dump[0] = p.t(g.dma_start(hb_dump.ap()[4 * og],
                                                      bbc[:, :]), dX, 16)
                    last_w1 = None
                    for tt in range(8):
                        bank = pb[4 + tt % 2]
                        p.w(te, tok_qh, wtoks[par], bank_free[tt % 2], tb1bc)
                        for ci in range(NCH):
                            p.t(te.matmul(
                                bank[:, :],
                                fA[:, ci * T + tt * 128:ci * T + (tt + 1) * 128],
                                wslot[par][:, ci * 512:(ci + 1) * 512],
                                start=(ci == 0), stop=(ci == NCH - 1),
                                skip_group_check=True), sTE)
                        mk = p.cur(sTE)
                        last_w1 = mk
                        if dbg2 and og == 0:
                            p.w(ve, mk)
                            td_ = p.t(ve.tensor_copy(h2f[:, :], bank[:, :]), sVE)
                            p.w(g, td_)
                            tdd = p.t(g.dma_start(w1_dump.ap()[tt], h2f[:, :]),
                                      dX, 16)
                            p.w(ve, tdd)
                        p.w(ve, mk, h1t_free, tb1bc, h1t_dump[0])
                        tcp = p.t(ve.tensor_add(h1t[:, :], bank[:, :], bbc[:, :]), sVE)
                        bank_free[tt % 2] = tcp
                        hsl = {(0, 0): 1, (0, 1): 2, (1, 4): 5, (1, 5): 6}
                        dtok = None
                        if dbg2 and (og, tt) in hsl:
                            p.w(g, tcp)
                            dtok = p.t(g.dma_start(hb_dump.ap()[hsl[(og, tt)]],
                                                   h1t[:, :]), dX, 16)
                        p.w(se, tcp, dtok)
                        tg_ = p.t(se.activation(h1t[:, :], h1t[:, :],
                                                AF.Gelu_apprx_tanh), sSE)
                        gsl = {(0, 1): 3, (1, 5): 7}
                        if dbg2 and (og, tt) in gsl:
                            p.w(g, tg_)
                            h1t_dump[0] = p.t(g.dma_start(
                                hb_dump.ap()[gsl[(og, tt)]], h1t[:, :]), dX, 16)
                        p.w(ve, tg_)
                        p.t(ve.tensor_reduce(
                            h2sc[:, :],
                            h1t.ap().rearrange("p (b w) -> p b w", w=32),
                            AX.X, ALU.max, apply_absolute_value=True), sVE)
                        p.t(ve.tensor_scalar(h2sc.ap().bitcast(I32),
                                             h2sc.ap().bitcast(I32),
                                             EXPMASK, None, ALU.bitwise_and), sVE)
                        p.t(ve.tensor_scalar(h2sc[:, :], h2sc[:, :], 0.015625,
                                             None, ALU.mult), sVE)
                        p.t(ve.reciprocal(qrsc[:, 0:16], h2sc[:, :]), sVE)
                        h1t_free = p.t(ve.tensor_mul(
                            h2f.ap().rearrange("p (b w) -> p b w", w=32),
                            h1t.ap().rearrange("p (b w) -> p b w", w=32),
                            qrsc.ap()[:, 0:16][:, :, None]
                            .broadcast_to([128, 16, 32])), sVE)
                        p.t(ve.tensor_scalar(h2f[:, :], h2f[:, :], MAGIC, -MAGIC,
                                             ALU.add, ALU.add), sVE)
                        p.w(ve, h2q_free)
                        thq = p.t(ve.scalar_tensor_tensor(
                            h2q.ap().rearrange("p (b w) -> p b w", w=32),
                            h2f.ap().rearrange("p (b w) -> p b w", w=32), 127.0,
                            h2sc.ap()[:, :, None]
                            .broadcast_to([128, 16, 32]),
                            ALU.min, ALU.mult), sVE)
                        p.w(sy, thq)
                        tsp = p.t(sy.dma_start(h2_spill.ap()[tt, og], h2q[:, :]),
                                  dSY, 16)
                        sp_tok[(tt, og)] = tsp
                        h2q_free = tsp
                    qw_free[par] = last_w1

                fence()
                y_tok = {}
                res_tok = None
                for tp in range(4):
                    tr_tok = None
                    gather_tok = None
                    for m in range(2):
                        tt = 2 * tp + m
                        p.w(sy, *[sp_tok[(tt, og)] for og in range(16)])
                        p.w(sy, tr_tok)
                        gather_tok = p.t(sy.dma_start(
                            fB.ap()[:, 0:HID].rearrange("p (a q) -> p a q", q=512),
                            h2_spill.ap()[tt].rearrange("a p q -> p a q")), dSY, 16)
                        for ci in range(64):
                            p.w(te, gather_tok, tr_tok)
                            tk = p.t(te.transpose(
                                pb[4 + ci % 2].ap().bitcast(BF16)[:, 0:128],
                                fB[:, ci * 128:(ci + 1) * 128],
                                idbf[:, :]), sTE)
                            p.w(ve, tk)
                            tr_tok = p.t(ve.tensor_copy(
                                fA[:, ci * 256 + m * 128:ci * 256 + (m + 1) * 128],
                                pb[4 + ci % 2].ap().bitcast(BF16)[:, 0:128]), sVE)
                    for o2 in range(4):
                        par = o2 % 2
                        banks = (pb[par], pb[2 + par])
                        for rc in range(16):
                            slot = 2 * par + rc % 2
                            qsem = dQW0 if rc % 2 == 0 else dQW1
                            wt = fB[:, HID + slot * 2048: HID + (slot + 1) * 2048]
                            p.w(sy, y_tok.get(("wfree", slot)),
                                y_tok.get(("lastmm", qsem.name)))
                            tw = p.t(sy.dma_start(
                                wt.rearrange("p (c q) -> p c q", q=512),
                                qw2_ag.ap()[rc * 512:(rc + 1) * 512,
                                            o2 * 512:(o2 + 1) * 512]
                                .rearrange("(c p) q -> p c q", p=128)), qsem, 16)
                            p.w(te, tw, tr_tok, y_tok.get(("ev", par, 0)),
                                y_tok.get(("ev", par, 1)))
                            last = None
                            for k in range(4):
                                ci = rc * 4 + k
                                for m in range(2):
                                    last = p.t(te.matmul(
                                        banks[m][:, :],
                                        fA[:, ci * 256 + m * 128:
                                           ci * 256 + (m + 1) * 128],
                                        wt[:, k * 512:(k + 1) * 512],
                                        start=(ci == 0), stop=(ci == 63),
                                        skip_group_check=True), sTE)
                            y_tok[("wfree", slot)] = last
                            y_tok[("lastmm", qsem.name)] = last
                        p.w(g, b2_free[0])
                        tb2 = p.t(g.dma_start(
                            b1t[:, :], din["b2_row"].ap()[:, o2 * 512:(o2 + 1) * 512]),
                            dX, 16)
                        p.w(te, tb2, b2_free[0])
                        tbm2 = p.t(te.matmul(pb[6][:, :], ones_rb[:, :], b1t[:, :],
                                             start=True, stop=True,
                                             skip_group_check=True), sTE)
                        p.w(ve, tbm2)
                        tb2bc = p.t(ve.tensor_copy(bbc[:, :], pb[6][:, :]), sVE)
                        b2_free[0] = tb2bc
                        mk = p.cur(sTE)
                        for m in range(2):
                            p.w(ve, mk, tb2bc)
                            y_tok[("ev", par, m)] = p.t(ve.tensor_add(
                                y2tm[:, m * D + o2 * 512:m * D + (o2 + 1) * 512],
                                banks[m][:, :], bbc[:, :]), sVE)
                    tok_y2 = p.cur(sSE)
                    for m in range(2):
                        tt = 2 * tp + m
                        for ci in range(NCH):
                            p.w(te, tok_y2, res_tok)
                            tk = p.t(te.transpose(
                                pb[7].ap().bitcast(BF16)[:, 0:128],
                                y2tm[:, m * D + ci * 128:m * D + (ci + 1) * 128],
                                idbf[:, :]), sTE)
                            cols = slice(ci * T + tt * 128, ci * T + (tt + 1) * 128)
                            p.w(ve, tk)
                            ta = p.t(ve.tensor_add(cbf[:, 0:128],
                                                   pb[7].ap().bitcast(BF16)[0:128, 0:128],
                                                   xb[:, cols]), sVE)
                            p.w(ve, ta)
                            res_tok = p.t(ve.tensor_add(xr[:, cols], xr[:, cols],
                                                        cbf[:, 0:128]), sVE)
                tok_res = p.cur(sVE)
                for ci in range(NCH):
                    p.w(ve, tok_res)
                    p.t(ve.tensor_copy(xb[:, ci * T:(ci + 1) * T],
                                       xr[:, ci * T:(ci + 1) * T]), sVE)
            return p.cur(sVE)

        tok_stream = tok_x
        tok_prep = None
        if stages >= 4:
            tok_prep = prep_quant()
            fence()

        t_a0 = attention(0, tok_stream)
        tok_ln = ln_apply(ln_stats(t_a0))
        fence()

        if stages >= 2:
            t_f0 = ffn0(tok_ln, dbg)
            tok_ln = ln_apply(ln_stats(t_f0))
            fence()

        if stages >= 3:
            t_a1 = attention(1, tok_ln)
            tok_ln = ln_apply(ln_stats(t_a1))
            fence()

        if stages >= 4:
            t_f1 = ffn1(tok_ln, tok_prep, dbg2)
            tok_ln = ln_apply(ln_stats(t_f1), refresh_xb=False)
            fence()

        if dbg2:
            fence()
            t1_ = p.t(g.dma_start(qw1_dump.ap()[:, :], qw1_ag.ap()[:, :]), dX, 16)
            t2_ = p.t(g.dma_start(sp_dump.ap()[:, :, :, :], h2_spill.ap()[:, :, :, :]), dX, 16)
            p.w(g, t1_, t2_)
        p.w(g, tok_ln, p.cur(sVE))
        outs = []
        for ci in range(NCH):
            outs.append(p.t(g.dma_start(y_out.ap()[ci],
                                        xr[:, ci * T:(ci + 1) * T]), dX, 16))
        p.w(g, *outs)

    return nc


_NC_CACHE = {}


def _host_layout(inputs):
    f32 = np.float32
    x = np.ascontiguousarray(inputs["L0_attn_x"][0])

    def tiles_co(w):
        return np.ascontiguousarray(w.reshape(NCH, 128, NCH, 128).transpose(2, 0, 1, 3))

    def tiles_ci4(w):
        return np.ascontiguousarray(w.reshape(NCH, 128, NCH, 128).transpose(0, 2, 1, 3))

    def bias_chunks(b):
        return np.ascontiguousarray(b.reshape(NCH, 128).astype(f32).T)

    com = {}
    com["ident32"] = np.eye(128, dtype=f32)
    fm = np.zeros((4, 128, 512), dtype=f32)
    for i in range(4):
        for j in range(4):
            if j < i:
                fm[i, :, j * 128:(j + 1) * 128] = NEG
            elif j == i:
                kk = np.arange(128)[:, None]
                qq = np.arange(128)[None, :]
                fm[i, :, j * 128:(j + 1) * 128] = np.where(kk > qq, NEG, 0.0)
    com["fixmask"] = fm
    for w, key in (("q", "wq"), ("k", "wk")):
        com[f"l0_w{w}_codes"] = tiles_co(inputs[f"L0_attn_{key}_codes"])
        com[f"l0_w{w}_scales"] = np.ascontiguousarray(
            inputs[f"L0_attn_{key}_scales"].reshape(64, NCH, 128).transpose(1, 0, 2))
    com["l0_wo_codes"] = tiles_ci4(inputs["L0_attn_wo_codes"])
    com["l0_wo_scales"] = np.ascontiguousarray(
        inputs["L0_attn_wo_scales"].reshape(NCH, 4, D))
    com["l0_wv_codes"] = np.ascontiguousarray(
        inputs["L0_attn_wv_codes"].reshape(NCH, 128, D))
    com["l0_wv_scales"] = np.ascontiguousarray(inputs["L0_attn_wv_scales"])
    for b in ("bq", "bk", "bv", "bo"):
        com[f"l0_{b}"] = bias_chunks(inputs[f"L0_attn_{b}"])
    com["l0_wg0"] = np.ascontiguousarray(
        inputs["L0_ffn_w_gating"][0].reshape(NCH, 128, 64, 128).transpose(2, 0, 1, 3))
    com["l0_wg1"] = np.ascontiguousarray(
        inputs["L0_ffn_w_gating"][1].reshape(NCH, 128, 64, 128).transpose(2, 0, 1, 3))
    com["l0_wl"] = np.ascontiguousarray(
        inputs["L0_ffn_w_linear"].reshape(64, 128, NCH, 128).transpose(2, 0, 1, 3))
    for w, key in (("q", "Wq"), ("k", "Wk")):
        com[f"l1_w{w}"] = tiles_co(inputs[f"L1_attn_{key}"])
    com["l1_wo"] = tiles_ci4(inputs["L1_attn_Wo"])
    com["l1_wv"] = np.ascontiguousarray(inputs["L1_attn_Wv"].reshape(NCH, 128, D))
    for b, key in (("bq", "bq"), ("bk", "bk"), ("bv", "bv"), ("bo", "bo")):
        com[f"l1_{b}"] = bias_chunks(inputs[f"L1_attn_{key}"])
    com["ln_w"] = bias_chunks(inputs["L1_ffn_ln_w"])
    com["ln_b"] = bias_chunks(inputs["L1_ffn_ln_b"])
    com["b1_row"] = np.ascontiguousarray(inputs["L1_ffn_b1"].reshape(1, HID))
    com["b2_row"] = np.ascontiguousarray(inputs["L1_ffn_b2"].reshape(1, D))
    W1, W2 = inputs["L1_ffn_W1"], inputs["L1_ffn_W2"]

    in_maps = []
    for c in range(NCORES):
        m = dict(com)
        sl = slice(T * c, T * (c + 1))
        m["x_fm"] = np.ascontiguousarray(x[sl].T.reshape(NCH, 128, T))
        posv = np.arange(T * c, T * (c + 1), dtype=f32)
        m["qpos"] = posv.reshape(1, T).copy()
        m["qpos_cols"] = np.ascontiguousarray(posv.reshape(8, 128).T)
        m["w1_nat_sl"] = np.ascontiguousarray(W1[:, 256 * c:256 * (c + 1)])
        m["w1t_sl"] = np.ascontiguousarray(W1[:, 256 * c:256 * (c + 1)].T)
        m["w2_nat_sl"] = np.ascontiguousarray(W2[:, 1024 * c:1024 * (c + 1)])
        m["w2t_sl"] = np.ascontiguousarray(W2[:, 1024 * c:1024 * (c + 1)].T)
        in_maps.append(m)
    return in_maps


def kernel(**inputs):
    key = (STAGES, DBG, DBG2)
    if key not in _NC_CACHE:
        _NC_CACHE[key] = build(STAGES, DBG, DBG2)
    nc = _NC_CACHE[key]
    in_maps = _host_layout(inputs)
    res = run_bass_kernel_spmd(nc, in_maps, core_ids=list(range(NCORES)))
    out = np.empty((1, S, D), dtype=np.float32)
    for c in range(NCORES):
        y = res.results[c]["y"]
        out[0, T * c:T * (c + 1), :] = y.transpose(2, 0, 1).reshape(T, D)
    if DBG:
        kernel._dbg = [res.results[c].get("dbg") for c in range(NCORES)]
    if DBG2:
        kernel._dbg2 = [{k: res.results[c].get(k) for k in
                         ("qw1_dump", "sp_dump", "w1_dump", "fa_dump",
                          "xb_dump", "ws_dump", "st_dump", "hb_dump")}
                        for c in range(NCORES)]
    return out


# version 04
# speedup vs baseline: 1.0205x; 1.0205x over previous
import math
from contextlib import ExitStack

import numpy as np
import ml_dtypes

import concourse.bass as bass
import concourse.mybir as mybir
from concourse.bass_utils import run_bass_kernel_spmd

F32 = mybir.dt.float32
BF16 = mybir.dt.bfloat16
F8 = mybir.dt.float8e4
I32 = mybir.dt.int32
AF = mybir.ActivationFunctionType
ALU = mybir.AluOpType
AX = mybir.AxisListType

NCORES = 8
T = 1024
D = 2048
NH = 16
HD = 128
S = 8192
HID = 8192
NCH = 16
NKB = 64
RS = 1.0 / math.sqrt(128.0)
NEG = -1.0e30
MAGIC = 12582912.0
KPR = 2 * D + 1
EXPMASK = 0x7F800000

STAGES = 4
DBG = False
DBG2 = False


class Sem:
    __slots__ = ("h", "n", "name")


class P:
    def __init__(self, nc, stack):
        self.nc = nc
        self.stack = stack
        self.waited = {}

    def sem(self, name):
        s = Sem()
        s.h = self.stack.enter_context(self.nc.semaphore(name))
        s.n = 0
        s.name = name
        return s

    def t(self, inst, s, k=1):
        if s.name in ("sVE", "sSE", "sGE") and s.n > 0:
            inst._wait_ge(s.h, s.n)
        inst.then_inc(s.h, k)
        s.n += k
        return (s, s.n)

    def w(self, eng, *toks):
        best = {}
        for tok in toks:
            if tok is None:
                continue
            s, v = tok
            if v <= 0:
                continue
            if best.get(s.name, (None, 0))[1] < v:
                best[s.name] = (s, v)
        for s, v in best.values():
            key = (id(eng), s.name)
            if self.waited.get(key, 0) >= v:
                continue
            eng.wait_ge(s.h, v)
            self.waited[key] = v

    def cur(self, s):
        return (s, s.n)


def build(stages=STAGES, dbg=False, dbg2=False):
    nc = bass.Bass("TRN2", target_bir_lowering=False)
    g, sy, te, ve, se = nc.gpsimd, nc.sync, nc.tensor, nc.vector, nc.scalar

    din = {}

    def dram_in(name, shape, dtype=F32):
        din[name] = nc.dram_tensor(name, list(shape), dtype, kind="ExternalInput")
        return din[name]

    dram_in("x_fm", [NCH, 128, T])
    dram_in("qpos", [1, T])
    dram_in("qpos_cols", [128, 8])
    dram_in("ident32", [128, 128])
    dram_in("fixmask", [4, 128, 512])
    for w in ("q", "k"):
        dram_in(f"l0_w{w}_codes", [NH, NCH, 128, 128])
        dram_in(f"l0_w{w}_scales", [NH, 64, 128])
    dram_in("l0_wo_codes", [NH, NCH, 128, 128])
    dram_in("l0_wo_scales", [NH, 4, D])
    dram_in("l0_wv_codes", [NCH, 128, D])
    dram_in("l0_wv_scales", [64, D])
    for b in ("bq", "bk", "bv", "bo"):
        dram_in(f"l0_{b}", [128, NCH])
    dram_in("l0_wg0", [64, NCH, 128, 128])
    dram_in("l0_wg1", [64, NCH, 128, 128])
    dram_in("l0_wl", [NCH, 64, 128, 128])
    for w in ("q", "k"):
        dram_in(f"l1_w{w}", [NH, NCH, 128, 128])
    dram_in("l1_wo", [NH, NCH, 128, 128])
    dram_in("l1_wv", [NCH, 128, D])
    for b in ("bq", "bk", "bv", "bo"):
        dram_in(f"l1_{b}", [128, NCH])
    dram_in("ln_w", [128, NCH])
    dram_in("ln_b", [128, NCH])
    dram_in("b1_row", [1, HID])
    dram_in("b2_row", [1, D])
    dram_in("w1_nat_sl", [HID, 256])
    dram_in("w1t_sl", [256, HID])
    dram_in("w2_nat_sl", [D, 1024])
    dram_in("w2t_sl", [1024, D])

    y_out = nc.dram_tensor("y", [NCH, 128, T], F32, kind="ExternalOutput")
    dbg_out = nc.dram_tensor("dbg", [128, 8192], F32, kind="ExternalOutput") if dbg else None
    if dbg2:
        qw1_dump = nc.dram_tensor("qw1_dump", [D, HID], BF16, kind="ExternalOutput")
        sp_dump = nc.dram_tensor("sp_dump", [8, 16, 128, 512], BF16, kind="ExternalOutput")
        w1_dump = nc.dram_tensor("w1_dump", [8, 128, 512], F32, kind="ExternalOutput")
        fa_dump = nc.dram_tensor("fa_dump", [128, 16384], BF16, kind="ExternalOutput")
        xb_dump = nc.dram_tensor("xb_dump", [128, 16384], BF16, kind="ExternalOutput")
        ws_dump = nc.dram_tensor("ws_dump", [128, 8192], BF16, kind="ExternalOutput")
        st_dump = nc.dram_tensor("st_dump", [2, 1024], F32, kind="ExternalOutput")
        hb_dump = nc.dram_tensor("hb_dump", [8, 128, 512], BF16, kind="ExternalOutput")

    kv_pack = [nc.dram_tensor(f"kv_pack{i}", [KPR, T], BF16) for i in (0, 1)]
    kv_ag = [nc.dram_tensor(f"kv_ag{i}", [NCORES * KPR, T], BF16,
                            addr_space="Shared") for i in (0, 1)]
    qw1_sl = nc.dram_tensor("qw1_sl", [256, HID], BF16)
    qw1_ag = nc.dram_tensor("qw1_ag", [D, HID], BF16, addr_space="Shared")
    qw2_sl = nc.dram_tensor("qw2_sl", [1024, D], BF16)
    qw2_ag = nc.dram_tensor("qw2_ag", [HID, D], BF16, addr_space="Shared")
    sc_bounce = nc.dram_tensor("sc_bounce", [32, HID], F32)
    aug_dram = nc.dram_tensor("aug_dram", [NH, 3, T], BF16)
    h2_spill = nc.dram_tensor("h2_spill", [8, 16, 128, 512], BF16)

    stack = ExitStack()
    p = P(nc, stack)

    with stack:
        sb = stack.enter_context
        xr = sb(nc.sbuf_tensor("xr", [128, NCH * T], F32))
        xb = sb(nc.sbuf_tensor("xb", [128, NCH * T], BF16))
        id32 = sb(nc.sbuf_tensor("id32", [128, 128], F32))
        idbf = sb(nc.sbuf_tensor("idbf", [128, 128], BF16))
        ones_cb = sb(nc.sbuf_tensor("ones_cb", [128, 1], BF16))
        ones_rf = sb(nc.sbuf_tensor("ones_rf", [1, 128], F32))
        ones_rb = sb(nc.sbuf_tensor("ones_rb", [1, 128], BF16))
        ones3 = sb(nc.sbuf_tensor("ones3", [3, 128], BF16))
        augrem = sb(nc.sbuf_tensor("augrem", [1, 512], F32))
        bias_sb = sb(nc.sbuf_tensor("bias_sb", [128, NCH * 8], F32))
        lnw_sb = sb(nc.sbuf_tensor("lnw_sb", [128, NCH], F32))
        lnb_sb = sb(nc.sbuf_tensor("lnb_sb", [128, NCH], F32))
        slopes_r = sb(nc.sbuf_tensor("slopes_r", [1, NH], F32))
        slope_c = sb(nc.sbuf_tensor("slope_c", [128, NH], F32))
        qpos_sb = sb(nc.sbuf_tensor("qpos_sb", [1, T], F32))
        qpm_sb = sb(nc.sbuf_tensor("qpm_sb", [128, 8], F32))
        kbrow = sb(nc.sbuf_tensor("kbrow", [1, NKB], F32))
        arow = sb(nc.sbuf_tensor("arow", [128, NKB], F32))
        knorm_r = sb(nc.sbuf_tensor("knorm_r", [1, NH], F32))
        kn2max = sb(nc.sbuf_tensor("kn2max", [1, NH], F32))
        normsc = sb(nc.sbuf_tensor("normsc", [1, 128], BF16))
        flagcol = sb(nc.sbuf_tensor("flagcol", [128, 2 * NKB], F32))
        rowtmp = sb(nc.sbuf_tensor("rowtmp", [1, NKB], F32))
        negc8 = sb(nc.sbuf_tensor("negc8", [128, 1], F32))
        biascol = sb(nc.sbuf_tensor("biascol", [128, 2 * NKB], F32))
        fixbias = sb(nc.sbuf_tensor("fixbias", [128, 8], F32))
        c2t = sb(nc.sbuf_tensor("c2t", [1, 1], F32))
        stat_m = sb(nc.sbuf_tensor("stat_m", [1, T], F32))
        stat_v = sb(nc.sbuf_tensor("stat_v", [1, T], F32))
        mrow2 = sb(nc.sbuf_tensor("mrow2", [1, T], F32))
        x2bf = sb(nc.sbuf_tensor("x2bf", [128, T], BF16))

        pb = [sb(nc.psum_tensor(f"pb{i}", [128, 512], F32)) for i in range(8)]

        sTE = p.sem("sTE")
        sVE = p.sem("sVE")
        sSE = p.sem("sSE")
        sGE = p.sem("sGE")
        sCC = p.sem("sCC")
        dX = p.sem("dX")
        dW0 = p.sem("dW0")
        dW1 = p.sem("dW1")
        dPK = p.sem("dPK")
        dPK1 = p.sem("dPK1")
        dPKV = p.sem("dPKV")
        dKVa = p.sem("dKVa")
        dKVb = p.sem("dKVb")
        dSY = p.sem("dSY")
        dQW0 = p.sem("dQW0")
        dQW1 = p.sem("dQW1")
        ALL_SEMS = (sTE, sVE, sSE, sGE, sCC, dX, dW0, dW1, dPK, dPK1, dPKV,
                    dKVa, dKVb, dSY, dQW0, dQW1)

        def fence():
            toks = [p.cur(s) for s in ALL_SEMS if s.n > 0]
            for eng in (te, ve, se, g, sy):
                p.w(eng, *toks)

        for ci in range(NCH):
            p.t(g.dma_start(xr[:, ci * T:(ci + 1) * T], din["x_fm"][ci]), dX, 16)
            p.t(g.dma_start(xb[:, ci * T:(ci + 1) * T], din["x_fm"][ci]), dX, 16)
        p.t(g.dma_start(id32[:, :], din["ident32"][:, :]), dX, 16)
        p.t(g.dma_start(idbf[:, :], din["ident32"][:, :]), dX, 16)
        p.t(g.dma_start(qpos_sb[:, :], din["qpos"][:, :]), dX, 16)
        for i, b in enumerate(("l0_bq", "l0_bk", "l0_bv", "l0_bo",
                               "l1_bq", "l1_bk", "l1_bv", "l1_bo")):
            p.t(g.dma_start(bias_sb[:, i * NCH:(i + 1) * NCH],
                            din[b][:, :]), dX, 16)
        p.t(g.dma_start(lnw_sb[:, :], din["ln_w"][:, :]), dX, 16)
        p.t(g.dma_start(lnb_sb[:, :], din["ln_b"][:, :]), dX, 16)
        p.t(g.dma_start(qpm_sb[:, :], din["qpos_cols"][:, :]), dX, 16)
        tok_x = p.cur(dX)

        p.t(g.memset(ones_cb[:, :], 1.0), sGE)
        p.t(g.memset(ones_rf[:, :], 1.0), sGE)
        p.t(g.memset(ones_rb[:, :], 1.0), sGE)
        p.t(g.memset(ones3[:, :], 1.0), sGE)
        p.t(g.iota(kbrow[:, :], [[128, NKB]], base=0, channel_multiplier=0,
                   allow_small_or_imprecise_dtypes=True), sGE)
        p.t(g.iota(arow[:, :], [[128, NKB]], base=0, channel_multiplier=1,
                   allow_small_or_imprecise_dtypes=True), sGE)
        p.t(g.iota(slopes_r[:, :], [[1, NH]], base=1, channel_multiplier=0,
                   allow_small_or_imprecise_dtypes=True), sGE)
        tok_ge0 = p.cur(sGE)
        p.w(se, tok_ge0)
        p.t(se.activation(slopes_r[:, :], slopes_r[:, :], AF.Exp,
                          scale=-math.log(2.0) / 2.0), sSE)
        tok_slopes = p.cur(sSE)
        tok_qpm = tok_x
        p.w(te, tok_slopes, tok_ge0)
        mk = p.t(te.matmul(pb[0][:, 0:NH], ones_rf[:, :], slopes_r[:, :],
                           start=True, stop=True, skip_group_check=True), sTE)
        p.w(ve, mk)
        p.t(ve.tensor_copy(slope_c[:, :], pb[0][:, 0:NH]), sVE)
        tok_slopec = p.cur(sVE)

        def ln_stats(tok_ready):
            p.w(te, tok_ready, p.cur(sVE), p.cur(sSE))
            for ci in range(NCH):
                for hf in range(2):
                    mk = p.t(te.matmul(pb[hf][0:1, :], ones_cb[:, :],
                                       xb[:, ci * T + hf * 512:ci * T + (hf + 1) * 512],
                                       start=(ci == 0), stop=(ci == NCH - 1),
                                       skip_group_check=True), sTE)
            tok_x2mm = None
            for ci in range(NCH):
                p.w(ve, tok_ready, tok_x2mm)
                tsq = p.t(ve.tensor_mul(x2bf[:, :], xb[:, ci * T:(ci + 1) * T],
                                        xb[:, ci * T:(ci + 1) * T]), sVE)
                p.w(te, tsq)
                for hf in range(2):
                    tok_x2mm = p.t(te.matmul(pb[2 + hf][0:1, :], ones_cb[:, :],
                                             x2bf[:, hf * 512:(hf + 1) * 512],
                                             start=(ci == 0), stop=(ci == NCH - 1),
                                             skip_group_check=True), sTE)
            tok_mm = p.cur(sTE)
            p.w(ve, tok_mm)
            p.t(ve.tensor_scalar(stat_m[:, 0:512], pb[0][0:1, :], 1.0 / D, None,
                                 ALU.mult), sVE)
            p.t(ve.tensor_scalar(stat_m[:, 512:1024], pb[1][0:1, :], 1.0 / D, None,
                                 ALU.mult), sVE)
            p.t(ve.tensor_scalar(stat_v[:, 0:512], pb[2][0:1, :], 1.0 / D, None,
                                 ALU.mult), sVE)
            p.t(ve.tensor_scalar(stat_v[:, 512:1024], pb[3][0:1, :], 1.0 / D, None,
                                 ALU.mult), sVE)
            p.t(ve.tensor_mul(mrow2[:, :], stat_m[:, :], stat_m[:, :]), sVE)
            p.t(ve.tensor_sub(stat_v[:, :], stat_v[:, :], mrow2[:, :]), sVE)
            p.t(ve.tensor_scalar(stat_v[:, :], stat_v[:, :], 1e-5, None, ALU.add), sVE)
            tok_v = p.cur(sVE)
            p.w(se, tok_v)
            p.t(se.activation(stat_v[:, :], stat_v[:, :], AF.Sqrt), sSE)
            tok_sq = p.cur(sSE)
            p.w(ve, tok_sq)
            p.t(ve.reciprocal(stat_v[:, :], stat_v[:, :]), sVE)
            return p.cur(sVE)

        def ln_apply(tok_stats, refresh_xb=True):
            for ci in range(NCH):
                for hf in range(2):
                    cols = slice(ci * T + hf * 512, ci * T + (hf + 1) * 512)
                    rows = slice(hf * 512, (hf + 1) * 512)
                    bm, bv_ = pb[4 + hf], pb[6 + hf]
                    p.w(te, tok_stats, p.cur(sVE))
                    t1 = p.t(te.matmul(bm[:, :], ones_rf[:, :], stat_m[:, rows],
                                       start=True, stop=True,
                                       skip_group_check=True), sTE)
                    t2 = p.t(te.matmul(bv_[:, :], ones_rf[:, :], stat_v[:, rows],
                                       start=True, stop=True,
                                       skip_group_check=True), sTE)
                    p.w(ve, t1, t2)
                    p.t(ve.tensor_sub(xr[:, cols], xr[:, cols], bm[:, :]), sVE)
                    p.t(ve.tensor_mul(xr[:, cols], xr[:, cols], bv_[:, :]), sVE)
                if refresh_xb:
                    tok_ap = p.cur(sVE)
                    p.w(ve, tok_ap)
                    p.t(ve.tensor_copy(xb[:, ci * T:(ci + 1) * T],
                                       xr[:, ci * T:(ci + 1) * T]), sVE)
            return p.cur(sVE)

        def attention(L, tok_stream):
            pack, ag = kv_pack[L], kv_ag[L]
            is_l0 = (L == 0)
            bq_c = bias_sb[:, (0 + 4 * L) * NCH:(1 + 4 * L) * NCH]
            bk_c = bias_sb[:, (1 + 4 * L) * NCH:(2 + 4 * L) * NCH]
            bv_c = bias_sb[:, (2 + 4 * L) * NCH:(3 + 4 * L) * NCH]
            bo_c = bias_sb[:, (3 + 4 * L) * NCH:(4 + 4 * L) * NCH]
            sfx = f"a{L}"

            with ExitStack() as ast:
                asb = ast.enter_context
                kh = asb(nc.sbuf_tensor(f"kh{sfx}", [128, S], BF16))
                vh = asb(nc.sbuf_tensor(f"vh{sfx}", [128, S], BF16))
                wcod = asb(nc.sbuf_tensor(f"wc{sfx}", [128, D], BF16))
                wscl = asb(nc.sbuf_tensor(f"ws{sfx}", [128, D], BF16))
                wb0 = asb(nc.sbuf_tensor(f"wb0{sfx}", [128, D], BF16))
                wb1 = asb(nc.sbuf_tensor(f"wb1{sfx}", [128, D], BF16))
                sta = asb(nc.sbuf_tensor(f"sta{sfx}", [128, T], BF16))
                stb = asb(nc.sbuf_tensor(f"stb{sfx}", [128, T], BF16))
                qsb = asb(nc.sbuf_tensor(f"q{sfx}", [128, T], BF16))
                psb = asb(nc.sbuf_tensor(f"p{sfx}", [128, 4 * 512], BF16))
                k2bf = asb(nc.sbuf_tensor(f"k2{sfx}", [128, T], BF16))
                osb = asb(nc.sbuf_tensor(f"o{sfx}", [128, T], BF16))
                sfix = asb(nc.sbuf_tensor(f"sf{sfx}", [128, 512], F32))
                kdiag = asb(nc.sbuf_tensor(f"kd{sfx}", [128, 128], BF16))
                fixm_sb = asb(nc.sbuf_tensor(f"fm{sfx}", [128, 4 * 512], BF16))
                aug_sb = asb(nc.sbuf_tensor(f"au{sfx}", [3, T], BF16))
                augr = asb(nc.sbuf_tensor(f"ar{sfx}", [1, 512], F32))
                augbf = asb(nc.sbuf_tensor(f"ab{sfx}", [1, T], BF16))
                augr2 = asb(nc.sbuf_tensor(f"ar2{sfx}", [1, 512], F32))
                rinv_r = asb(nc.sbuf_tensor(f"ri{sfx}", [1, 512], F32))
                rinv_bf = asb(nc.sbuf_tensor(f"rb{sfx}", [1, 512], BF16))
                rinv_bc = asb(nc.sbuf_tensor(f"rc{sfx}", [128, 512], BF16))
                vdiag = asb(nc.sbuf_tensor(f"vd{sfx}", [128, 128], BF16))
                trtile = asb(nc.sbuf_tensor(f"tr{sfx}", [128, 128], BF16))
                tok_fixm = None
                if not is_l0:
                    tok_fixm = p.t(g.dma_start(fixm_sb[:, :],
                                               din["fixmask"].ap()
                                               .rearrange("a p q -> p a q")), dX, 16)
                wfree = {0: None, 1: None}
                deq_free = [None]

                def fetch_colblock(w, h, par):
                    dst = wb0 if par == 0 else wb1
                    wsem = dW0 if par == 0 else dW1
                    p.w(g, wfree[par], deq_free[0])
                    if is_l0:
                        if w == "v":
                            src = din["l0_wv_codes"].ap()[:, :, h * 128:(h + 1) * 128] \
                                .rearrange("c p d -> p c d")
                            sbase = din["l0_wv_scales"].ap()[:, h * 128:(h + 1) * 128] \
                                .rearrange("(c a) d -> a c d", a=4)
                        elif w == "o":
                            src = din["l0_wo_codes"].ap()[h].rearrange("c p d -> p c d")
                            sbase = din["l0_wo_scales"].ap()[h]
                        else:
                            src = din[f"l0_w{w}_codes"].ap()[h].rearrange("c p d -> p c d")
                            sbase = din[f"l0_w{w}_scales"].ap()[h] \
                                .rearrange("(c a) d -> a c d", a=4)
                        t0 = p.t(g.dma_start(wcod[:, :], src), wsem, 16)
                        t1 = None
                        for a in range(4):
                            if w == "o":
                                ss = sbase[a][None, :].broadcast_to([32, D])
                                od = wscl[32 * a:32 * (a + 1), :]
                            else:
                                ss = sbase[a][None, :, :].broadcast_to([32, NCH, 128])
                                od = wscl[32 * a:32 * (a + 1), :] \
                                    .rearrange("p (c d) -> p c d", d=128)
                            t1 = p.t(g.dma_start(od, ss), wsem, 16)
                        p.w(ve, t0, t1)
                        tok = p.t(ve.tensor_mul(dst[:, :], wcod[:, :], wscl[:, :]), sVE)
                        deq_free[0] = tok
                        return tok
                    else:
                        if w == "v":
                            src = din["l1_wv"].ap()[:, :, h * 128:(h + 1) * 128] \
                                .rearrange("c p d -> p c d")
                        elif w == "o":
                            src = din["l1_wo"].ap()[h].rearrange("c p d -> p c d")
                        else:
                            src = din[f"l1_w{w}"].ap()[h].rearrange("c p d -> p c d")
                        return p.t(g.dma_start(dst[:, :], src), wsem, 16)

                def proj_head(wname, h, par, tok_in):
                    tok_w = fetch_colblock(wname, h, par)
                    p.w(te, tok_w, tok_in, p.cur(sSE), p.cur(sVE))
                    dst = wb0 if par == 0 else wb1
                    last = None
                    for hf in range(2):
                        bank = pb[par] if hf == 0 else pb[2 + par]
                        for ci in range(NCH):
                            last = p.t(te.matmul(
                                bank[:, :], dst[:, ci * 128:(ci + 1) * 128],
                                xb[:, ci * T + hf * 512:ci * T + (hf + 1) * 512],
                                start=(ci == 0), stop=(ci == NCH - 1),
                                skip_group_check=True), sTE)
                    wfree[par] = last
                    return last

                def head_norm2(src_bf, dstcol):
                    tk = p.t(ve.tensor_mul(k2bf[:, :], src_bf[:, :], src_bf[:, :]), sVE)
                    p.w(te, tk)
                    t1 = p.t(te.matmul(pb[4][0:1, :], ones_cb[:, :], k2bf[:, 0:512],
                                       start=True, stop=True, skip_group_check=True), sTE)
                    t2 = p.t(te.matmul(pb[5][0:1, :], ones_cb[:, :], k2bf[:, 512:1024],
                                       start=True, stop=True, skip_group_check=True), sTE)
                    p.w(ve, t1, t2)
                    p.t(ve.tensor_reduce(rowtmp[:, 0:1], pb[4][0:1, :], AX.X,
                                         ALU.max), sVE)
                    p.t(ve.tensor_reduce(rowtmp[:, 1:2], pb[5][0:1, :], AX.X,
                                         ALU.max), sVE)
                    return p.t(ve.tensor_max(dstcol, rowtmp[:, 0:1],
                                             rowtmp[:, 1:2]), sVE)

                stg_free = {0: None, 1: None}
                for co in range(NH):
                    par = co % 2
                    stg = sta if par == 0 else stb
                    last = proj_head("k", co, par, tok_stream)
                    p.w(se, last, stg_free[par])
                    p.t(se.activation(stg[:, 0:512], pb[par][:, :], AF.Identity,
                                      bias=bk_c[:, co:co + 1]), sSE)
                    tk = p.t(se.activation(stg[:, 512:1024], pb[2 + par][:, :],
                                           AF.Identity, bias=bk_c[:, co:co + 1]), sSE)
                    p.w(g, tk)
                    stg_free[par] = p.t(
                        g.dma_start(pack.ap()[co * 128:(co + 1) * 128, :],
                                    stg[:, :]), dPK if par == 0 else dPK1, 16)
                    p.w(ve, tk)
                    head_norm2(stg, knorm_r[:, co:co + 1])

                vtr_free = {0: None, 1: None}
                vpk_free = None
                for co in range(NH):
                    par = co % 2
                    stg = sta if par == 0 else stb
                    last = proj_head("v", co, par, tok_stream)
                    p.w(se, last, stg_free[par], vtr_free[par])
                    p.t(se.activation(stg[:, 0:512], pb[par][:, :], AF.Copy), sSE)
                    tk = p.t(se.activation(stg[:, 512:1024], pb[2 + par][:, :],
                                           AF.Copy), sSE)
                    for tt in range(8):
                        p.w(te, tk, p.cur(sVE))
                        tq = p.t(te.transpose(pb[6].ap().bitcast(BF16)[:, 0:128],
                                              stg[:, tt * 128:(tt + 1) * 128],
                                              idbf[:, :]), sTE)
                        vtr_free[par] = tq
                        p.w(ve, tq, vpk_free)
                        tv = p.t(ve.tensor_copy(trtile[:, :],
                                                pb[6].ap().bitcast(BF16)[:, 0:128]), sVE)
                        p.w(g, tv)
                        vpk_free = p.t(g.dma_start(
                            pack.ap()[D + co * 128 + tt * 16:
                                      D + co * 128 + tt * 16 + 16, :]
                            .rearrange("r t -> (r t)")
                            .rearrange("(q d) -> q d", d=128),
                            trtile[:, :]), dPKV, 16)

                p.w(ve, p.cur(sVE), vpk_free)
                p.t(ve.memset(osb[0:1, :], 0.0), sVE)
                tn = p.t(ve.tensor_copy(osb[0:1, 0:NH], knorm_r[:, :]), sVE)
                p.w(g, tn)
                p.t(g.dma_start(pack.ap()[2 * D:2 * D + 1, :],
                                osb[0:1, :]), dPKV, 16)

                p.w(g, p.cur(dPK), p.cur(dPK1), p.cur(dPKV))
                tok_ag = p.t(g.collective_compute(
                    "AllGather", ALU.bypass,
                    replica_groups=[list(range(NCORES))],
                    ins=[pack.ap().opt()], outs=[ag.ap().opt()]), sCC)

                p.w(sy, tok_ag)
                tn = p.t(sy.dma_start(
                    normsc[:, :],
                    ag.ap().rearrange("(r q) t -> r q t", r=NCORES)
                    [:, 2 * D:2 * D + 1, 0:NH]
                    .rearrange("r q t -> q r t")), dSY, 16)
                p.w(ve, tn)
                tok_kn = p.t(ve.tensor_reduce(
                    kn2max[:, :],
                    normsc.ap().rearrange("p (r h) -> p h r", r=NCORES),
                    AX.X, ALU.max), sVE)

                tok_flag = None
                if not is_l0:
                    for gg in range(2):
                        p.w(ve, tok_x, tok_ge0)
                        p.t(ve.tensor_scalar(rowtmp[:, :], kbrow[:, :],
                                             qpos_sb[0:1, 0:1], -512.0 * gg,
                                             ALU.subtract, ALU.add), sVE)
                        tf = p.t(ve.tensor_scalar(rowtmp[:, :], rowtmp[:, :],
                                                  0.0, NEG, ALU.is_ge, ALU.mult), sVE)
                        p.w(te, tf)
                        mk = p.t(te.matmul(pb[7][:, 0:NKB], ones_rf[:, :],
                                           rowtmp[:, :], start=True, stop=True,
                                           skip_group_check=True), sTE)
                        p.w(ve, mk)
                        p.t(ve.tensor_copy(flagcol[:, gg * NKB:(gg + 1) * NKB],
                                           pb[7][:, 0:NKB]), sVE)
                    tok_flag = p.cur(sVE)

                tok_aug_pack = None
                if not is_l0:
                    for h2 in range(NH):
                        p.w(ve, tok_slopes, tok_x)
                        p.t(ve.tensor_scalar(augr[:, 0:T] if False else augr[:, :],
                                             qpos_sb[0:1, 0:512],
                                             slopes_r[0:1, h2:h2 + 1], -1.0 / RS,
                                             ALU.mult, ALU.mult), sVE)
                        p.t(ve.tensor_scalar(augr2[:, :], qpos_sb[0:1, 512:1024],
                                             slopes_r[0:1, h2:h2 + 1], -1.0 / RS,
                                             ALU.mult, ALU.mult), sVE)
                        for j in range(3):
                            p.w(ve, tok_aug_pack)
                            p.t(ve.tensor_copy(augbf[:, 0:512], augr[:, :]), sVE)
                            p.t(ve.tensor_copy(augbf[:, 512:1024], augr2[:, :]), sVE)
                            tj = p.cur(sVE)
                            p.w(g, tj)
                            tok_aug_pack = p.t(
                                g.dma_start(aug_dram.ap()[h2, j], augbf[:, :]),
                                dPKV, 16)
                            if j < 2:
                                p.t(ve.tensor_sub(augr[:, :], augr[:, :],
                                                  augbf[:, 0:512]), sVE)
                                p.t(ve.tensor_sub(augr2[:, :], augr2[:, :],
                                                  augbf[:, 512:1024]), sVE)

                def load_kv_half(h, half, ksem):
                    for r in range(4 * half, 4 * half + 4):
                        base = r * KPR
                        p.t(sy.dma_start(
                            kh[:, r * T:(r + 1) * T],
                            ag.ap()[base + h * 128:base + (h + 1) * 128, :]),
                            ksem, 16)
                        p.t(sy.dma_start(
                            vh.ap()[:, r * T:(r + 1) * T]
                            .rearrange("p (tb d) -> p tb d", d=128),
                            ag.ap()[base + D + h * 128:base + D + (h + 1) * 128, :]
                            .rearrange("r t -> (r t)")
                            .rearrange("(tb tp d) -> tp tb d", tb=8, d=128)),
                            ksem, 16)
                    return p.cur(ksem)

                p.w(sy, tok_ag)
                kv_tok = [load_kv_half(0, 0, dKVa), load_kv_half(0, 1, dKVb)]

                exp_tok = {}
                for h in range(NH):
                    p.w(te, p.cur(sVE), p.cur(sSE))
                    last = proj_head("q", h, 0, tok_stream)
                    p.w(se, last)
                    p.t(se.activation(qsb[:, 0:512], pb[0][:, :], AF.Identity,
                                      bias=bq_c[:, h:h + 1]), sSE)
                    tok_q = p.t(se.activation(qsb[:, 512:1024], pb[2][:, :],
                                              AF.Identity, bias=bq_c[:, h:h + 1]), sSE)
                    p.w(ve, tok_q, tok_kn)
                    head_norm2(qsb, c2t[:, :])
                    p.t(ve.tensor_mul(c2t[:, :], c2t[:, :], kn2max[:, h:h + 1]), sVE)
                    tok_c2 = p.cur(sVE)
                    p.w(se, tok_c2)
                    tc = p.t(se.activation(c2t[:, :], c2t[:, :], AF.Sqrt,
                                           scale=(1.02 * RS) ** 2), sSE)
                    p.w(ve, tc)
                    tnc = p.t(ve.tensor_scalar(c2t[:, :], c2t[:, :], -1.0, 6.0,
                                               ALU.mult, ALU.add), sVE)
                    p.w(te, tnc)
                    mk = p.t(te.matmul(pb[6][:, 0:1], ones_rf[:, :], c2t[:, :],
                                       start=True, stop=True,
                                       skip_group_check=True), sTE)
                    p.w(ve, mk)
                    tok_nc = p.t(ve.tensor_copy(negc8[:, :], pb[6][:, 0:1]), sVE)

                    p.w(ve, tok_nc, tok_slopec, tok_flag, p.cur(sSE))
                    if is_l0:
                        for gg in range(2):
                            p.t(ve.tensor_copy(
                                biascol[:, gg * NKB:(gg + 1) * NKB],
                                negc8.ap().broadcast_to([128, NKB])), sVE)
                    else:
                        for gg in range(2):
                            p.t(ve.scalar_tensor_tensor(
                                biascol[:, gg * NKB:(gg + 1) * NKB],
                                arow[:, :], slope_c[:, h:h + 1],
                                negc8.ap().broadcast_to([128, NKB]),
                                ALU.mult, ALU.add), sVE)
                            p.t(ve.tensor_add(
                                biascol[:, gg * NKB:(gg + 1) * NKB],
                                biascol[:, gg * NKB:(gg + 1) * NKB],
                                flagcol[:, gg * NKB:(gg + 1) * NKB]), sVE)
                        p.w(ve, tok_qpm)
                        p.t(ve.scalar_tensor_tensor(
                            fixbias[:, :], qpm_sb[:, :], slope_c[:, h:h + 1],
                            negc8.ap().broadcast_to([128, 8]),
                            ALU.mult, ALU.add), sVE)
                    tok_bias = p.cur(sVE)
                    if not is_l0:
                        p.w(sy, tok_aug_pack, p.cur(sTE))
                        tok_aug = p.t(sy.dma_start(
                            aug_sb[:, :],
                            aug_dram.ap()[h].rearrange("a t -> a t")), dSY, 16)
                        p.w(te, tok_aug)
                    p.w(se, tok_bias)
                    p.w(te, tok_bias)

                    for kb in range(NKB):
                        ktok = kv_tok[kb // 32]
                        for gg in range(2):
                            sbank = pb[2 * gg + (kb % 2)]
                            p.w(te, ktok, exp_tok.get((gg, kb % 2)), tok_q)
                            if not is_l0:
                                p.t(te.matmul(sbank[:, :], ones3[:, :],
                                              aug_sb[:, gg * 512:(gg + 1) * 512],
                                              start=True, stop=False,
                                              skip_group_check=True), sTE)
                            ts_ = p.t(te.matmul(sbank[:, :],
                                                kh[:, kb * 128:(kb + 1) * 128],
                                                qsb[:, gg * 512:(gg + 1) * 512],
                                                start=is_l0, stop=True,
                                                skip_group_check=True), sTE)
                            pslot = slice((2 * gg + kb % 2) * 512,
                                          (2 * gg + kb % 2 + 1) * 512)
                            p.w(se, ts_)
                            tke = p.t(se.activation(
                                psb[:, pslot], sbank[:, :], AF.Exp,
                                bias=biascol[:, gg * NKB + kb:gg * NKB + kb + 1],
                                scale=RS), sSE)
                            exp_tok[(gg, kb % 2)] = tke
                            p.w(te, tke)
                            p.t(te.matmul(pb[4 + gg][:, :],
                                          vh[:, kb * 128:(kb + 1) * 128],
                                          psb[:, pslot], start=(kb == 0),
                                          stop=(is_l0 and kb == NKB - 1),
                                          skip_group_check=True), sTE)
                            p.t(te.matmul(pb[6 + gg][0:1, :], ones_cb[:, :],
                                          psb[:, pslot], start=(kb == 0),
                                          stop=(is_l0 and kb == NKB - 1),
                                          skip_group_check=True), sTE)
                        if kb == 31 and h + 1 < NH:
                            p.w(sy, p.cur(sTE))
                            kv_tok[0] = load_kv_half(h + 1, 0, dKVa)
                    if h + 1 < NH:
                        p.w(sy, p.cur(sTE))
                        kv_tok[1] = load_kv_half(h + 1, 1, dKVb)

                    if not is_l0:
                        for gg in range(2):
                            for i in range(4):
                                tb = 4 * gg + i
                                p.w(sy, p.cur(sTE))
                                tdk = p.t(sy.dma_start(
                                    kdiag[:, :],
                                    pack.ap()[h * 128:(h + 1) * 128,
                                              tb * 128:(tb + 1) * 128]), dSY, 16)
                                tdv = p.t(sy.dma_start(
                                    vdiag[:, :],
                                    pack.ap()[D + h * 128 + tb * 16:
                                              D + h * 128 + tb * 16 + 16, :]
                                    .rearrange("r t -> (r t)")
                                    .rearrange("(tp d) -> tp d", d=128)), dSY, 16)
                                p.w(te, tdk, tdv, tok_q, exp_tok.get((gg, 0)))
                                p.t(te.matmul(pb[2 * gg][:, :], ones3[:, :],
                                              aug_sb[:, gg * 512:(gg + 1) * 512],
                                              start=True, stop=False,
                                              skip_group_check=True), sTE)
                                ts_ = p.t(te.matmul(pb[2 * gg][:, :], kdiag[:, :],
                                                    qsb[:, gg * 512:(gg + 1) * 512],
                                                    start=False, stop=True,
                                                    skip_group_check=True), sTE)
                                p.w(ve, ts_, p.cur(sSE), tok_fixm)
                                tm = p.t(ve.tensor_add(
                                    sfix[:, :], pb[2 * gg][:, :],
                                    fixm_sb[:, i * 512:(i + 1) * 512]), sVE)
                                p.w(se, tm)
                                tke = p.t(se.activation(
                                    psb[:, (2 * gg) * 512:(2 * gg + 1) * 512],
                                    sfix[:, :], AF.Exp,
                                    bias=fixbias[:, tb:tb + 1], scale=RS), sSE)
                                exp_tok[(gg, 0)] = tke
                                p.w(te, tke, tdv)
                                p.t(te.matmul(pb[4 + gg][:, :], vdiag[:, :],
                                              psb[:, (2 * gg) * 512:
                                                  (2 * gg + 1) * 512],
                                              start=False,
                                              stop=(gg == 1 and i == 3),
                                              skip_group_check=True), sTE)
                                p.t(te.matmul(pb[6 + gg][0:1, :], ones_cb[:, :],
                                              psb[:, (2 * gg) * 512:
                                                  (2 * gg + 1) * 512],
                                              start=False,
                                              stop=(gg == 1 and i == 3),
                                              skip_group_check=True), sTE)
                    tok_pv = p.cur(sTE)

                    prev_rinv_mm = None
                    for gg in range(2):
                        p.w(ve, tok_pv, prev_rinv_mm)
                        p.t(ve.reciprocal(rinv_r[:, :], pb[6 + gg][0:1, :]), sVE)
                        tr_ = p.t(ve.tensor_copy(rinv_bf[:, :], rinv_r[:, :]), sVE)
                        p.w(te, tr_)
                        mk = p.t(te.matmul(pb[2 * gg][:, :], ones_rb[:, :],
                                           rinv_bf[:, :], start=True, stop=True,
                                           skip_group_check=True), sTE)
                        prev_rinv_mm = mk
                        p.w(ve, mk)
                        p.t(ve.tensor_copy(rinv_bc[:, :], pb[2 * gg][:, :]), sVE)
                        tro = p.t(ve.tensor_mul(
                            k2bf[:, gg * 512:(gg + 1) * 512],
                            pb[4 + gg][:, :], rinv_bc[:, :]), sVE)
                        p.w(se, tro)
                        p.t(se.activation(osb[:, gg * 512:(gg + 1) * 512],
                                          k2bf[:, gg * 512:(gg + 1) * 512],
                                          AF.Identity, bias=bv_c[:, h:h + 1]), sSE)
                    tok_o = p.cur(sSE)

                    tok_wo = fetch_colblock("o", h, 1)
                    p.w(te, tok_wo, tok_o, p.cur(sVE))
                    for oc in range(NCH):
                        for gg in range(2):
                            bank = pb[2 * gg + (oc % 2)]
                            p.w(te, exp_tok.get(("wo", oc % 2, gg)))
                            mk = p.t(te.matmul(bank[:, :],
                                               wb1[:, oc * 128:(oc + 1) * 128],
                                               osb[:, gg * 512:(gg + 1) * 512],
                                               start=True, stop=True,
                                               skip_group_check=True), sTE)
                            p.w(ve, mk)
                            cols = slice(oc * T + gg * 512, oc * T + (gg + 1) * 512)
                            ta = p.t(ve.tensor_add(xr[:, cols], xr[:, cols],
                                                   bank[:, :]), sVE)
                            exp_tok[("wo", oc % 2, gg)] = ta
                    wfree[1] = p.cur(sTE)

                if dbg and L == 0:
                    with nc.sbuf_tensor("dbgs", [128, 4096], F32) as dbgs:
                        p.w(ve, p.cur(sVE), p.cur(sSE), p.cur(sTE))
                        p.t(ve.memset(dbgs[:, :], 0.0), sVE)
                        p.t(ve.tensor_copy(dbgs[:, 0:1024], xr[:, 0:1024]), sVE)
                        p.t(ve.tensor_copy(dbgs[:, 1024:2048], osb[:, :]), sVE)
                        p.t(ve.tensor_copy(dbgs[0:1, 2048:2064], kn2max[:, :]), sVE)
                        p.t(ve.tensor_copy(dbgs[0:1, 2064:2080], knorm_r[:, :]), sVE)
                        p.t(ve.tensor_copy(dbgs[:, 2080:2081], negc8[:, :]), sVE)
                        p.t(ve.tensor_copy(dbgs[0:1, 2100:2612], rinv_r[:, :]), sVE)
                        p.t(ve.tensor_copy(dbgs[:, 3072:4096], qsb[:, :]), sVE)
                        p.t(ve.tensor_copy(dbgs[:, 2612:2676], biascol[:, 0:64]), sVE)
                        p.t(ve.tensor_copy(dbgs[0:1, 2676:2804],
                                           normsc[:, :]), sVE)
                        tdbg = p.cur(sVE)
                        p.w(g, tdbg)
                        p.t(g.dma_start(dbg_out.ap()[:, 0:4096], dbgs[:, :]), dX, 16)
                        p.w(g, p.cur(dX))
                p.w(ve, p.cur(sVE))
                for ci in range(NCH):
                    p.t(ve.tensor_scalar(xr[:, ci * T:(ci + 1) * T],
                                         xr[:, ci * T:(ci + 1) * T],
                                         bo_c[:, ci:ci + 1], None, ALU.add), sVE)
                tok_res = p.cur(sVE)
                for ci in range(NCH):
                    p.w(ve, tok_res)
                    p.t(ve.tensor_copy(xb[:, ci * T:(ci + 1) * T],
                                       xr[:, ci * T:(ci + 1) * T]), sVE)
            return p.cur(sVE)

        def ffn0(tok_stream, dbg=False):
            with (
                nc.sbuf_tensor("xf8", [128, NCH * T], F8) as xf8,
                nc.sbuf_tensor("actf8", [128, 64 * 512], F8) as actf8,
                nc.sbuf_tensor("w8a", [128, D], F8) as w8a,
                nc.sbuf_tensor("w8b", [128, D], F8) as w8b,
                nc.sbuf_tensor("w8c", [128, D], F8) as w8c,
                nc.sbuf_tensor("w8d", [128, D], F8) as w8d,
                nc.sbuf_tensor("gbf", [128, 512], BF16) as gbf,
                nc.sbuf_tensor("fbf", [128, 512], BF16) as fbf,
                nc.sbuf_tensor("abf", [128, 512], BF16) as abf,
            ):
                for ci in range(NCH):
                    p.w(ve, tok_stream)
                    p.t(ve.tensor_copy(xf8[:, ci * T:(ci + 1) * T],
                                       xr[:, ci * T:(ci + 1) * T]), sVE)
                tok_x8 = p.cur(sVE)

                wg_free = {0: None, 1: None}
                wf_free = {0: None, 1: None}
                ev = {}
                tok_wl_done = None
                for tg in range(2):
                    for hc in range(64):
                        par = hc % 2
                        wsem = dW0 if par == 0 else dW1
                        wg_ = w8a if par == 0 else w8b
                        wf_ = w8c if par == 0 else w8d
                        p.w(g, wg_free[par])
                        t0 = p.t(g.dma_start(
                            wg_[:, :], din["l0_wg0"].ap()[hc]
                            .rearrange("c p q -> p c q")), wsem, 16)
                        p.w(g, wf_free[par])
                        t1 = p.t(g.dma_start(
                            wf_[:, :], din["l0_wg1"].ap()[hc]
                            .rearrange("c p q -> p c q")), wsem, 16)
                        bg, bf_ = pb[par], pb[2 + par]
                        p.w(te, t0, t1, tok_x8, ev.get(("g", par)), ev.get(("f", par)))
                        gm = fm = None
                        for ci in range(NCH):
                            gm = p.t(te.matmul(
                                bg[:, :], wg_[:, ci * 128:(ci + 1) * 128],
                                xf8[:, ci * T + tg * 512:ci * T + (tg + 1) * 512],
                                start=(ci == 0), stop=(ci == NCH - 1),
                                skip_group_check=True), sTE)
                        for ci in range(NCH):
                            fm = p.t(te.matmul(
                                bf_[:, :], wf_[:, ci * 128:(ci + 1) * 128],
                                xf8[:, ci * T + tg * 512:ci * T + (tg + 1) * 512],
                                start=(ci == 0), stop=(ci == NCH - 1),
                                skip_group_check=True), sTE)
                        wg_free[par] = gm
                        wf_free[par] = fm
                        p.w(se, gm, p.cur(sVE))
                        p.t(se.activation(gbf[:, :], bg[:, :], AF.Copy), sSE)
                        p.t(se.activation(gbf[:, :], gbf[:, :],
                                          AF.Gelu_apprx_tanh), sSE)
                        p.w(se, fm)
                        tf = p.t(se.activation(fbf[:, :], bf_[:, :], AF.Copy), sSE)
                        ev[("g", par)] = tf
                        ev[("f", par)] = tf
                        p.w(ve, tf, tok_wl_done)
                        p.t(ve.tensor_mul(abf[:, :], gbf[:, :], fbf[:, :]), sVE)
                        ev[("a", hc)] = p.t(ve.tensor_copy(
                            actf8[:, hc * 512:(hc + 1) * 512], abf[:, :]), sVE)
                        if dbg and tg == 0 and hc == 0:
                            p.w(ve, p.cur(sVE), p.cur(sSE))
                            p.t(ve.tensor_copy(xf8[:, T:T + 1024].bitcast(BF16)
                                               .bitcast(F8)[:, 0:1024]
                                               if False else
                                               actf8[:, 63 * 512:64 * 512],
                                               abf[:, :]), sVE)
                            with nc.sbuf_tensor("dbg0", [128, 2048], F32) as dbg0:
                                p.t(ve.tensor_copy(dbg0[:, 0:512], gbf[:, :]), sVE)
                                p.t(ve.tensor_copy(dbg0[:, 512:1024], fbf[:, :]), sVE)
                                p.t(ve.tensor_copy(dbg0[:, 1024:1536], abf[:, :]), sVE)
                                p.t(ve.tensor_copy(dbg0[:, 1536:2048],
                                                   xf8[:, 0:512]), sVE)
                                tdbg = p.cur(sVE)
                                p.w(g, tdbg)
                                p.t(g.dma_start(dbg_out.ap()[:, 4096:6144],
                                                dbg0[:, :]), dX, 16)
                                p.w(g, p.cur(dX))
                    for oc in range(NCH):
                        par = oc % 2
                        wsem = dW0 if par == 0 else dW1
                        wl_ = w8a if par == 0 else w8b
                        bank = pb[4 + par]
                        om = None
                        for hg in range(4):
                            p.w(g, wg_free[par])
                            tw = p.t(g.dma_start(
                                wl_[:, :],
                                din["l0_wl"].ap()[oc, 16 * hg:16 * (hg + 1)]
                                .rearrange("c p q -> p c q")), wsem, 16)
                            p.w(te, tw, ev.get(("o", par)))
                            for k in range(16):
                                hc = hg * 16 + k
                                p.w(te, ev[("a", hc)])
                                om = p.t(te.matmul(
                                    bank[:, :], wl_[:, k * 128:(k + 1) * 128],
                                    actf8[:, hc * 512:(hc + 1) * 512],
                                    start=(hc == 0), stop=(hc == 63),
                                    skip_group_check=True), sTE)
                            wg_free[par] = om
                        p.w(se, om, p.cur(sVE))
                        to = p.t(se.activation(gbf[:, :],
                                               bank[:, :], AF.Copy), sSE)
                        p.w(ve, to)
                        cols = slice(oc * T + tg * 512, oc * T + (tg + 1) * 512)
                        ev[("o", par)] = p.t(ve.tensor_add(
                            xr[:, cols], xr[:, cols], gbf[:, :]), sVE)
                    tok_wl_done = p.cur(sTE)
                tok_res = p.cur(sVE)
                for ci in range(NCH):
                    p.w(ve, tok_res)
                    p.t(ve.tensor_copy(xb[:, ci * T:(ci + 1) * T],
                                       xr[:, ci * T:(ci + 1) * T]), sVE)
            return p.cur(sVE)

        def prep_quant():
            with (
                nc.sbuf_tensor("pnat", [128, 1024], F32) as pnat,
                nc.sbuf_tensor("pamax", [128, 32], F32) as pamax,
                nc.sbuf_tensor("pscT", [32, 128], F32) as pscT,
                nc.sbuf_tensor("papp", [128, 2048], F32) as papp,
                nc.sbuf_tensor("pbc", [128, 2048], F32) as pbc,
                nc.sbuf_tensor("prbc", [128, 2048], F32) as prbc,
                nc.sbuf_tensor("pout", [128, 2048], BF16) as pout,
            ):
                ag_jobs = []
                bounce_tok = [None]
                for (nat, tsl, qsl, qag, KSL, OD) in (
                    ("w1_nat_sl", "w1t_sl", qw1_sl, qw1_ag, 256, HID),
                    ("w2_nat_sl", "w2t_sl", qw2_sl, qw2_ag, 1024, D),
                ):
                    nblk = KSL // 32
                    prev_rd = None
                    tr_prev = None
                    for ot in range(OD // 128):
                        p.w(g, prev_rd)
                        tw = p.t(g.dma_start(
                            pnat[:, 0:KSL],
                            din[nat].ap()[ot * 128:(ot + 1) * 128, :]), dX, 16)
                        p.w(ve, tw, tr_prev)
                        prev_rd = p.t(ve.tensor_reduce(
                            pamax[:, 0:nblk],
                            pnat.ap()[:, 0:KSL].rearrange("p (b w) -> p b w", w=32),
                            AX.X, ALU.max, apply_absolute_value=True), sVE)
                        p.t(ve.tensor_scalar(
                            pamax.ap()[:, 0:nblk].bitcast(I32),
                            pamax.ap()[:, 0:nblk].bitcast(I32),
                            EXPMASK, None, ALU.bitwise_and), sVE)
                        tsc = p.t(ve.tensor_scalar(pamax[:, 0:nblk], pamax[:, 0:nblk],
                                                   0.015625, None, ALU.mult), sVE)
                        p.w(te, tsc)
                        tq = p.t(te.transpose(pb[6][0:nblk, 0:128],
                                              pamax[:, 0:nblk],
                                              id32[:, :]), sTE)
                        p.w(ve, tq, bounce_tok[0])
                        tr_prev = p.t(ve.tensor_copy(
                            pscT[0:nblk, 0:128],
                            pb[6][0:nblk, 0:128]), sVE)
                        p.w(g, tr_prev)
                        bounce_tok[0] = p.t(g.dma_start(
                            sc_bounce.ap()[0:nblk, ot * 128:(ot + 1) * 128],
                            pscT[0:nblk, 0:128]), dPK, 16)
                    tb = bounce_tok[0]
                    agt = []
                    for kc in range(KSL // 128):
                        for oh in range(OD // 2048):
                            oco = slice(oh * 2048, (oh + 1) * 2048)
                            p.w(g, tb, agt[-1] if agt else None)
                            t0 = p.t(g.dma_start(
                                papp[:, :],
                                din[tsl].ap()[kc * 128:(kc + 1) * 128, oco]), dX, 16)
                            t1 = None
                            for a in range(4):
                                t1 = p.t(g.dma_start(
                                    pbc[32 * a:32 * (a + 1), :],
                                    sc_bounce.ap()[kc * 4 + a][None, :][:, oco]
                                    .broadcast_to([32, 2048])), dX, 16)
                            p.w(ve, t0, t1)
                            p.t(ve.reciprocal(prbc[:, :], pbc[:, :]), sVE)
                            p.t(ve.tensor_mul(papp[:, :], papp[:, :], prbc[:, :]), sVE)
                            p.t(ve.tensor_scalar(papp[:, :], papp[:, :],
                                                 MAGIC, -MAGIC, ALU.add, ALU.add), sVE)
                            tq = p.t(ve.scalar_tensor_tensor(
                                pout[:, :], papp[:, :], 127.0, pbc[:, :],
                                ALU.min, ALU.mult), sVE)
                            p.w(g, tq)
                            agt.append(p.t(g.dma_start(
                                qsl.ap()[kc * 128:(kc + 1) * 128, oco],
                                pout[:, :]), dPK, 16))
                    ag_jobs.append((agt[-1], qsl, qag))
            return ag_jobs

        def ffn1(tok_stream, tok_prep, dbg2=False):
            with (
                nc.sbuf_tensor("fA", [128, NCH * T], BF16) as fA,
                nc.sbuf_tensor("fB", [128, NCH * T], BF16) as fB,
                nc.sbuf_tensor("cbf", [128, D], BF16) as cbf,
                nc.sbuf_tensor("qt32", [128, D // 2], F32) as qt32,
                nc.sbuf_tensor("qsc", [128, 64], F32) as qsc,
                nc.sbuf_tensor("qrsc", [128, 64], F32) as qrsc,
                nc.sbuf_tensor("h1t", [128, 512], BF16) as h1t,
                nc.sbuf_tensor("h2q", [128, 512], BF16) as h2q,
                nc.sbuf_tensor("h2f", [128, 512], F32) as h2f,
                nc.sbuf_tensor("h2sc", [128, 16], F32) as h2sc,
                nc.sbuf_tensor("y2tm", [128, 2 * D], BF16) as y2tm,
                nc.sbuf_tensor("b1t", [1, 512], BF16) as b1t,
                nc.sbuf_tensor("bbc", [128, 512], BF16) as bbc,
            ):
                if dbg2:
                    p.w(g, tok_stream)
                    p.t(g.dma_start(xb_dump.ap()[:, :], xb[:, :]), dX, 16)
                tok_stats = ln_stats(tok_stream)
                if dbg2:
                    p.w(g, tok_stats)
                    p.t(g.dma_start(st_dump.ap()[0:1, :], stat_m[:, :]), dX, 16)
                    p.t(g.dma_start(st_dump.ap()[1:2, :], stat_v[:, :]), dX, 16)
                tr_tok = None
                for ci in range(NCH):
                    for hf in range(2):
                        cols = slice(ci * T + hf * 512, ci * T + (hf + 1) * 512)
                        rows = slice(hf * 512, (hf + 1) * 512)
                        ccols = slice(hf * 512, (hf + 1) * 512)
                        bm, bv_ = pb[0 + hf], pb[2 + hf]
                        p.w(te, tok_stats, p.cur(sVE), p.cur(sSE))
                        t1 = p.t(te.matmul(bm[:, :], ones_rf[:, :], stat_m[:, rows],
                                           start=True, stop=True,
                                           skip_group_check=True), sTE)
                        t2 = p.t(te.matmul(bv_[:, :], ones_rf[:, :], stat_v[:, rows],
                                           start=True, stop=True,
                                           skip_group_check=True), sTE)
                        p.w(ve, t1, t2)
                        p.t(ve.tensor_sub(cbf[:, ccols], xb[:, cols], bm[:, :]), sVE)
                        tap = p.t(ve.tensor_mul(cbf[:, ccols], cbf[:, ccols],
                                                bv_[:, :]), sVE)
                        p.w(se, tap)
                        p.t(se.activation(cbf[:, ccols], cbf[:, ccols], AF.Identity,
                                          bias=lnb_sb[:, ci:ci + 1],
                                          scale=lnw_sb[:, ci:ci + 1]), sSE)
                    th = p.cur(sSE)
                    for tt in range(8):
                        p.w(te, th, tr_tok)
                        tq = p.t(te.transpose(
                            pb[4 + tt % 2].ap().bitcast(BF16)[:, 0:128],
                            cbf[:, tt * 128:(tt + 1) * 128],
                            idbf[:, :]), sTE)
                        p.w(ve, tq)
                        tr_tok = p.t(ve.tensor_copy(
                            fB[:, tt * D + ci * 128:tt * D + (ci + 1) * 128],
                            pb[4 + tt % 2].ap().bitcast(BF16)[:, 0:128]), sVE)
                for tt in range(8):
                    p.w(ve, tr_tok)
                    p.t(ve.tensor_reduce(
                        qsc[:, :],
                        fB.ap()[:, tt * D:(tt + 1) * D]
                        .rearrange("p (b w) -> p b w", w=32),
                        AX.X, ALU.max, apply_absolute_value=True), sVE)
                    p.t(ve.tensor_scalar(qsc.ap().bitcast(I32), qsc.ap().bitcast(I32),
                                         EXPMASK, None, ALU.bitwise_and), sVE)
                    p.t(ve.tensor_scalar(qsc[:, :], qsc[:, :], 0.015625, None,
                                         ALU.mult), sVE)
                    tsc = p.t(ve.reciprocal(qrsc[:, :], qsc[:, :]), sVE)
                    p.w(ve, tsc)
                    for hh in range(2):
                        hc2 = slice(hh * 1024, (hh + 1) * 1024)
                        hb = slice(hh * 32, (hh + 1) * 32)
                        p.t(ve.tensor_mul(
                            qt32.ap().rearrange("p (b w) -> p b w", w=32),
                            fB.ap()[:, tt * D:(tt + 1) * D][:, hc2]
                            .rearrange("p (b w) -> p b w", w=32),
                            qrsc.ap()[:, hb][:, :, None]
                            .broadcast_to([128, 32, 32])), sVE)
                        p.t(ve.tensor_scalar(qt32[:, :], qt32[:, :], MAGIC, -MAGIC,
                                             ALU.add, ALU.add), sVE)
                        tq = p.t(ve.scalar_tensor_tensor(
                            cbf.ap()[:, 0:D][:, hc2]
                            .rearrange("p (b w) -> p b w", w=32),
                            qt32.ap().rearrange("p (b w) -> p b w", w=32), 127.0,
                            qsc.ap()[:, hb][:, :, None]
                            .broadcast_to([128, 32, 32]),
                            ALU.min, ALU.mult), sVE)
                    for ci in range(NCH):
                        p.w(te, tq, p.cur(sVE))
                        tk = p.t(te.transpose(
                            pb[4 + ci % 2].ap().bitcast(BF16)[:, 0:128],
                            cbf[:, ci * 128:(ci + 1) * 128],
                            idbf[:, :]), sTE)
                        p.w(ve, tk)
                        p.t(ve.tensor_copy(
                            fA[:, ci * T + tt * 128:ci * T + (tt + 1) * 128],
                            pb[4 + ci % 2].ap().bitcast(BF16)[:, 0:128]), sVE)
                tok_qh = p.cur(sVE)
                if dbg2:
                    p.w(g, tok_qh)
                    p.t(g.dma_start(fa_dump.ap()[:, :], fA[:, :]), dX, 16)

                p.w(sy, tok_prep)
                wslot = [fB[:, 0:16 * 512], fB[:, 16 * 512:32 * 512]]
                qw_free = {0: p.cur(sVE), 1: p.cur(sVE)}
                b1_free = [None]
                b2_free = [None]
                wtoks = {}
                sp_tok = {}
                bank_free = {0: None, 1: None}
                h1t_free = None
                h2q_free = None
                ws_dump_tok = None
                bbc_dump = [None]
                h1t_dump = [None]
                for og in range(16):
                    par = og % 2
                    qsem = dQW0 if par == 0 else dQW1
                    p.w(sy, qw_free[par], tok_prep,
                        ws_dump_tok if og == 2 else None)
                    wtoks[par] = p.t(sy.dma_start(
                        wslot[par].rearrange("p (c q) -> p c q", q=512),
                        qw1_ag.ap()[:, og * 512:(og + 1) * 512]
                        .rearrange("(c p) q -> p c q", p=128)), qsem, 16)
                    if dbg2 and og == 0:
                        p.w(g, wtoks[par])
                        ws_dump_tok = p.t(
                            g.dma_start(ws_dump.ap()[:, :], wslot[0][:, :]),
                            dX, 16)
                    p.w(g, b1_free[0])
                    tb1 = p.t(g.dma_start(
                        b1t[:, :], din["b1_row"].ap()[:, og * 512:(og + 1) * 512]),
                        dX, 16)
                    p.w(te, tb1, b1_free[0])
                    tbm = p.t(te.matmul(pb[6][:, :], ones_rb[:, :], b1t[:, :],
                                        start=True, stop=True,
                                        skip_group_check=True), sTE)
                    p.w(ve, tbm, bbc_dump[0])
                    tb1bc = p.t(ve.tensor_copy(bbc[:, :], pb[6][:, :]), sVE)
                    b1_free[0] = tbm
                    if dbg2 and og in (0, 1):
                        p.w(g, tb1bc)
                        bbc_dump[0] = p.t(g.dma_start(hb_dump.ap()[4 * og],
                                                      bbc[:, :]), dX, 16)
                    last_w1 = None
                    for tt in range(8):
                        bank = pb[4 + tt % 2]
                        p.w(te, tok_qh, wtoks[par], bank_free[tt % 2], tb1bc)
                        for ci in range(NCH):
                            p.t(te.matmul(
                                bank[:, :],
                                fA[:, ci * T + tt * 128:ci * T + (tt + 1) * 128],
                                wslot[par][:, ci * 512:(ci + 1) * 512],
                                start=(ci == 0), stop=(ci == NCH - 1),
                                skip_group_check=True), sTE)
                        mk = p.cur(sTE)
                        last_w1 = mk
                        if dbg2 and og == 0:
                            p.w(ve, mk)
                            td_ = p.t(ve.tensor_copy(h2f[:, :], bank[:, :]), sVE)
                            p.w(g, td_)
                            tdd = p.t(g.dma_start(w1_dump.ap()[tt], h2f[:, :]),
                                      dX, 16)
                            p.w(ve, tdd)
                        p.w(ve, mk, h1t_free, tb1bc, h1t_dump[0])
                        tcp = p.t(ve.tensor_add(h1t[:, :], bank[:, :], bbc[:, :]), sVE)
                        bank_free[tt % 2] = tcp
                        hsl = {(0, 0): 1, (0, 1): 2, (1, 4): 5, (1, 5): 6}
                        dtok = None
                        if dbg2 and (og, tt) in hsl:
                            p.w(g, tcp)
                            dtok = p.t(g.dma_start(hb_dump.ap()[hsl[(og, tt)]],
                                                   h1t[:, :]), dX, 16)
                        p.w(se, tcp, dtok)
                        tg_ = p.t(se.activation(h1t[:, :], h1t[:, :],
                                                AF.Gelu_apprx_tanh), sSE)
                        gsl = {(0, 1): 3, (1, 5): 7}
                        if dbg2 and (og, tt) in gsl:
                            p.w(g, tg_)
                            h1t_dump[0] = p.t(g.dma_start(
                                hb_dump.ap()[gsl[(og, tt)]], h1t[:, :]), dX, 16)
                        p.w(ve, tg_)
                        p.t(ve.tensor_reduce(
                            h2sc[:, :],
                            h1t.ap().rearrange("p (b w) -> p b w", w=32),
                            AX.X, ALU.max, apply_absolute_value=True), sVE)
                        p.t(ve.tensor_scalar(h2sc.ap().bitcast(I32),
                                             h2sc.ap().bitcast(I32),
                                             EXPMASK, None, ALU.bitwise_and), sVE)
                        p.t(ve.tensor_scalar(h2sc[:, :], h2sc[:, :], 0.015625,
                                             None, ALU.mult), sVE)
                        p.t(ve.reciprocal(qrsc[:, 0:16], h2sc[:, :]), sVE)
                        h1t_free = p.t(ve.tensor_mul(
                            h2f.ap().rearrange("p (b w) -> p b w", w=32),
                            h1t.ap().rearrange("p (b w) -> p b w", w=32),
                            qrsc.ap()[:, 0:16][:, :, None]
                            .broadcast_to([128, 16, 32])), sVE)
                        p.t(ve.tensor_scalar(h2f[:, :], h2f[:, :], MAGIC, -MAGIC,
                                             ALU.add, ALU.add), sVE)
                        p.w(ve, h2q_free)
                        thq = p.t(ve.scalar_tensor_tensor(
                            h2q.ap().rearrange("p (b w) -> p b w", w=32),
                            h2f.ap().rearrange("p (b w) -> p b w", w=32), 127.0,
                            h2sc.ap()[:, :, None]
                            .broadcast_to([128, 16, 32]),
                            ALU.min, ALU.mult), sVE)
                        p.w(sy, thq)
                        tsp = p.t(sy.dma_start(h2_spill.ap()[tt, og], h2q[:, :]),
                                  dSY, 16)
                        sp_tok[(tt, og)] = tsp
                        h2q_free = tsp
                    qw_free[par] = last_w1

                fence()
                y_tok = {}
                res_tok = None
                for tp in range(4):
                    tr_tok = None
                    gather_tok = None
                    for m in range(2):
                        tt = 2 * tp + m
                        p.w(sy, *[sp_tok[(tt, og)] for og in range(16)])
                        p.w(sy, tr_tok)
                        gather_tok = p.t(sy.dma_start(
                            fB.ap()[:, 0:HID].rearrange("p (a q) -> p a q", q=512),
                            h2_spill.ap()[tt].rearrange("a p q -> p a q")), dSY, 16)
                        for ci in range(64):
                            p.w(te, gather_tok, tr_tok)
                            tk = p.t(te.transpose(
                                pb[4 + ci % 2].ap().bitcast(BF16)[:, 0:128],
                                fB[:, ci * 128:(ci + 1) * 128],
                                idbf[:, :]), sTE)
                            p.w(ve, tk)
                            tr_tok = p.t(ve.tensor_copy(
                                fA[:, ci * 256 + m * 128:ci * 256 + (m + 1) * 128],
                                pb[4 + ci % 2].ap().bitcast(BF16)[:, 0:128]), sVE)
                    for o2 in range(4):
                        par = o2 % 2
                        banks = (pb[par], pb[2 + par])
                        for rc in range(16):
                            slot = 2 * par + rc % 2
                            qsem = dQW0 if rc % 2 == 0 else dQW1
                            wt = fB[:, HID + slot * 2048: HID + (slot + 1) * 2048]
                            p.w(sy, y_tok.get(("wfree", slot)),
                                y_tok.get(("lastmm", qsem.name)))
                            tw = p.t(sy.dma_start(
                                wt.rearrange("p (c q) -> p c q", q=512),
                                qw2_ag.ap()[rc * 512:(rc + 1) * 512,
                                            o2 * 512:(o2 + 1) * 512]
                                .rearrange("(c p) q -> p c q", p=128)), qsem, 16)
                            p.w(te, tw, tr_tok, y_tok.get(("ev", par, 0)),
                                y_tok.get(("ev", par, 1)))
                            last = None
                            for k in range(4):
                                ci = rc * 4 + k
                                for m in range(2):
                                    last = p.t(te.matmul(
                                        banks[m][:, :],
                                        fA[:, ci * 256 + m * 128:
                                           ci * 256 + (m + 1) * 128],
                                        wt[:, k * 512:(k + 1) * 512],
                                        start=(ci == 0), stop=(ci == 63),
                                        skip_group_check=True), sTE)
                            y_tok[("wfree", slot)] = last
                            y_tok[("lastmm", qsem.name)] = last
                        p.w(g, b2_free[0])
                        tb2 = p.t(g.dma_start(
                            b1t[:, :], din["b2_row"].ap()[:, o2 * 512:(o2 + 1) * 512]),
                            dX, 16)
                        p.w(te, tb2, b2_free[0])
                        tbm2 = p.t(te.matmul(pb[6][:, :], ones_rb[:, :], b1t[:, :],
                                             start=True, stop=True,
                                             skip_group_check=True), sTE)
                        p.w(ve, tbm2)
                        tb2bc = p.t(ve.tensor_copy(bbc[:, :], pb[6][:, :]), sVE)
                        b2_free[0] = tb2bc
                        mk = p.cur(sTE)
                        for m in range(2):
                            p.w(ve, mk, tb2bc)
                            y_tok[("ev", par, m)] = p.t(ve.tensor_add(
                                y2tm[:, m * D + o2 * 512:m * D + (o2 + 1) * 512],
                                banks[m][:, :], bbc[:, :]), sVE)
                    tok_y2 = p.cur(sSE)
                    for m in range(2):
                        tt = 2 * tp + m
                        for ci in range(NCH):
                            p.w(te, tok_y2, res_tok)
                            tk = p.t(te.transpose(
                                pb[7].ap().bitcast(BF16)[:, 0:128],
                                y2tm[:, m * D + ci * 128:m * D + (ci + 1) * 128],
                                idbf[:, :]), sTE)
                            cols = slice(ci * T + tt * 128, ci * T + (tt + 1) * 128)
                            p.w(ve, tk)
                            ta = p.t(ve.tensor_add(cbf[:, 0:128],
                                                   pb[7].ap().bitcast(BF16)[0:128, 0:128],
                                                   xb[:, cols]), sVE)
                            p.w(ve, ta)
                            res_tok = p.t(ve.tensor_add(xr[:, cols], xr[:, cols],
                                                        cbf[:, 0:128]), sVE)
                tok_res = p.cur(sVE)
                for ci in range(NCH):
                    p.w(ve, tok_res)
                    p.t(ve.tensor_copy(xb[:, ci * T:(ci + 1) * T],
                                       xr[:, ci * T:(ci + 1) * T]), sVE)
            return p.cur(sVE)

        tok_stream = tok_x
        tok_prep = None
        ag_jobs = []
        if stages >= 4:
            ag_jobs = prep_quant()
            fence()

        t_a0 = attention(0, tok_stream)
        tok_ln = ln_apply(ln_stats(t_a0))
        fence()
        for (wtok, qsl_, qag_) in ag_jobs:
            p.w(g, wtok)
            tok_prep = p.t(g.collective_compute(
                "AllGather", ALU.bypass,
                replica_groups=[list(range(NCORES))],
                ins=[qsl_.ap().opt()], outs=[qag_.ap().opt()]), sCC)

        if stages >= 2:
            t_f0 = ffn0(tok_ln, dbg)
            tok_ln = ln_apply(ln_stats(t_f0))
            fence()

        if stages >= 3:
            t_a1 = attention(1, tok_ln)
            tok_ln = ln_apply(ln_stats(t_a1))
            fence()

        if stages >= 4:
            t_f1 = ffn1(tok_ln, tok_prep, dbg2)
            tok_ln = ln_apply(ln_stats(t_f1), refresh_xb=False)
            fence()

        if dbg2:
            fence()
            t1_ = p.t(g.dma_start(qw1_dump.ap()[:, :], qw1_ag.ap()[:, :]), dX, 16)
            t2_ = p.t(g.dma_start(sp_dump.ap()[:, :, :, :], h2_spill.ap()[:, :, :, :]), dX, 16)
            p.w(g, t1_, t2_)
        p.w(g, tok_ln, p.cur(sVE))
        outs = []
        for ci in range(NCH):
            outs.append(p.t(g.dma_start(y_out.ap()[ci],
                                        xr[:, ci * T:(ci + 1) * T]), dX, 16))
        p.w(g, *outs)

    return nc


_NC_CACHE = {}


def _host_layout(inputs):
    f32 = np.float32
    x = np.ascontiguousarray(inputs["L0_attn_x"][0])

    def tiles_co(w):
        return np.ascontiguousarray(w.reshape(NCH, 128, NCH, 128).transpose(2, 0, 1, 3))

    def tiles_ci4(w):
        return np.ascontiguousarray(w.reshape(NCH, 128, NCH, 128).transpose(0, 2, 1, 3))

    def bias_chunks(b):
        return np.ascontiguousarray(b.reshape(NCH, 128).astype(f32).T)

    com = {}
    com["ident32"] = np.eye(128, dtype=f32)
    fm = np.zeros((4, 128, 512), dtype=f32)
    for i in range(4):
        for j in range(4):
            if j < i:
                fm[i, :, j * 128:(j + 1) * 128] = NEG
            elif j == i:
                kk = np.arange(128)[:, None]
                qq = np.arange(128)[None, :]
                fm[i, :, j * 128:(j + 1) * 128] = np.where(kk > qq, NEG, 0.0)
    com["fixmask"] = fm
    for w, key in (("q", "wq"), ("k", "wk")):
        com[f"l0_w{w}_codes"] = tiles_co(inputs[f"L0_attn_{key}_codes"])
        com[f"l0_w{w}_scales"] = np.ascontiguousarray(
            inputs[f"L0_attn_{key}_scales"].reshape(64, NCH, 128).transpose(1, 0, 2))
    com["l0_wo_codes"] = tiles_ci4(inputs["L0_attn_wo_codes"])
    com["l0_wo_scales"] = np.ascontiguousarray(
        inputs["L0_attn_wo_scales"].reshape(NCH, 4, D))
    com["l0_wv_codes"] = np.ascontiguousarray(
        inputs["L0_attn_wv_codes"].reshape(NCH, 128, D))
    com["l0_wv_scales"] = np.ascontiguousarray(inputs["L0_attn_wv_scales"])
    for b in ("bq", "bk", "bv", "bo"):
        com[f"l0_{b}"] = bias_chunks(inputs[f"L0_attn_{b}"])
    com["l0_wg0"] = np.ascontiguousarray(
        inputs["L0_ffn_w_gating"][0].reshape(NCH, 128, 64, 128).transpose(2, 0, 1, 3))
    com["l0_wg1"] = np.ascontiguousarray(
        inputs["L0_ffn_w_gating"][1].reshape(NCH, 128, 64, 128).transpose(2, 0, 1, 3))
    com["l0_wl"] = np.ascontiguousarray(
        inputs["L0_ffn_w_linear"].reshape(64, 128, NCH, 128).transpose(2, 0, 1, 3))
    for w, key in (("q", "Wq"), ("k", "Wk")):
        com[f"l1_w{w}"] = tiles_co(inputs[f"L1_attn_{key}"])
    com["l1_wo"] = tiles_ci4(inputs["L1_attn_Wo"])
    com["l1_wv"] = np.ascontiguousarray(inputs["L1_attn_Wv"].reshape(NCH, 128, D))
    for b, key in (("bq", "bq"), ("bk", "bk"), ("bv", "bv"), ("bo", "bo")):
        com[f"l1_{b}"] = bias_chunks(inputs[f"L1_attn_{key}"])
    com["ln_w"] = bias_chunks(inputs["L1_ffn_ln_w"])
    com["ln_b"] = bias_chunks(inputs["L1_ffn_ln_b"])
    com["b1_row"] = np.ascontiguousarray(inputs["L1_ffn_b1"].reshape(1, HID))
    com["b2_row"] = np.ascontiguousarray(inputs["L1_ffn_b2"].reshape(1, D))
    W1, W2 = inputs["L1_ffn_W1"], inputs["L1_ffn_W2"]

    in_maps = []
    for c in range(NCORES):
        m = dict(com)
        sl = slice(T * c, T * (c + 1))
        m["x_fm"] = np.ascontiguousarray(x[sl].T.reshape(NCH, 128, T))
        posv = np.arange(T * c, T * (c + 1), dtype=f32)
        m["qpos"] = posv.reshape(1, T).copy()
        m["qpos_cols"] = np.ascontiguousarray(posv.reshape(8, 128).T)
        m["w1_nat_sl"] = np.ascontiguousarray(W1[:, 256 * c:256 * (c + 1)])
        m["w1t_sl"] = np.ascontiguousarray(W1[:, 256 * c:256 * (c + 1)].T)
        m["w2_nat_sl"] = np.ascontiguousarray(W2[:, 1024 * c:1024 * (c + 1)])
        m["w2t_sl"] = np.ascontiguousarray(W2[:, 1024 * c:1024 * (c + 1)].T)
        in_maps.append(m)
    return in_maps


def kernel(**inputs):
    key = (STAGES, DBG, DBG2)
    if key not in _NC_CACHE:
        _NC_CACHE[key] = build(STAGES, DBG, DBG2)
    nc = _NC_CACHE[key]
    in_maps = _host_layout(inputs)
    res = run_bass_kernel_spmd(nc, in_maps, core_ids=list(range(NCORES)))
    out = np.empty((1, S, D), dtype=np.float32)
    for c in range(NCORES):
        y = res.results[c]["y"]
        out[0, T * c:T * (c + 1), :] = y.transpose(2, 0, 1).reshape(T, D)
    if DBG:
        kernel._dbg = [res.results[c].get("dbg") for c in range(NCORES)]
    if DBG2:
        kernel._dbg2 = [{k: res.results[c].get(k) for k in
                         ("qw1_dump", "sp_dump", "w1_dump", "fa_dump",
                          "xb_dump", "ws_dump", "st_dump", "hb_dump")}
                        for c in range(NCORES)]
    return out
